# version 03
# baseline (speedup 1.0000x reference)
import math

import numpy as np
import concourse.bass as bass
import concourse.mybir as mybir
from concourse.bass_utils import run_bass_kernel_spmd
from concourse.replica_groups import maybe_share_collective_output_space

F32 = mybir.dt.float32
BF16 = mybir.dt.bfloat16
I32 = mybir.dt.int32
AF = mybir.ActivationFunctionType
ALU = mybir.AluOpType
AX = mybir.AxisListType.X

S = 4096
HID = 2048
NH = 4
D = 128
FH = 5632
NTT = S // 128
NHC = FH // 128
NDC = HID // 128
OWN = 1024

ISD = 1.0 / math.sqrt(D)
TWO_PI = 2.0 * math.pi
C_EPS = 1e-5
C_PI2 = math.pi / 2.0
C_NEG1 = -1.0
C_IVF = -2.0 * math.log(1.0e6) / D
NEG_BIG = -1.0e30

ENGS = ("tensor", "vector", "scalar", "gpsimd", "sync")


class Sem:
    __slots__ = ("h", "n", "name")

    def __init__(self, h, name):
        self.h = h
        self.n = 0
        self.name = name


class B:

    def __init__(self, dbg=()):
        self.dbg = set(dbg)
        self.nc = bass.Bass("TRN2", target_bir_lowering=False, detect_race_conditions=False)
        self.steps = {e: [] for e in ENGS}
        self._ctxs = []
        self.sems = {}
        self.dbg_outs = []

    def sem(self, name):
        if name not in self.sems:
            g = self.nc.semaphore(name)
            h = g.__enter__()
            self._ctxs.append(g)
            self.sems[name] = Sem(h, name)
        return self.sems[name]

    def sbuf(self, name, shape, dtype):
        g = self.nc.sbuf_tensor(name, shape, dtype)
        t = g.__enter__()
        self._ctxs.append(g)
        return t

    def psum(self, name, shape, dtype):
        g = self.nc.psum_tensor(name, shape, dtype)
        t = g.__enter__()
        self._ctxs.append(g)
        return t

    def step(self, eng, emit, waits=(), inc=None, amt=1, self_inc=False):
        ws = tuple((s, v) for (s, v) in waits if s is not None and v > 0)
        tok = None
        if inc is not None:
            inc.n += amt
            tok = (inc, inc.n)
        self.steps[eng].append((ws, emit, (inc.h if inc else None), amt, self_inc))
        return tok

    def replay(self, eng_name, eng):
        fence = False
        for ws, emit, inch, amt, self_inc in self.steps[eng_name]:
            for (s, v) in ws:
                eng.wait_ge(s.h, v)
            if self_inc:
                emit(eng, inch)
            else:
                inst = emit(eng)
                if inch is not None:
                    assert inst is not None, f"step on {eng_name} with inc returned None"
                    if fence:
                        inst = eng.drain()
                    inst.then_inc(inch, amt)


def build(dbg=(), stage="full", cut=99):
    b = B(dbg)
    nc = b.nc

    x_hbm = nc.dram_tensor("x", [S, HID], F32, kind="ExternalInput").ap()
    xown_hbm = nc.dram_tensor("x_own", [OWN, HID], F32, kind="ExternalInput").ap()
    wqkv_hbm = nc.dram_tensor("wqkv_t", [HID, 768], F32, kind="ExternalInput").ap()
    wo_hbm = nc.dram_tensor("wo_t", [512, HID], F32, kind="ExternalInput").ap()
    qng_hbm = nc.dram_tensor("qn_g", [1, D], F32, kind="ExternalInput").ap()
    qnb_hbm = nc.dram_tensor("qn_b", [1, D], F32, kind="ExternalInput").ap()
    kng_hbm = nc.dram_tensor("kn_g", [1, D], F32, kind="ExternalInput").ap()
    knb_hbm = nc.dram_tensor("kn_b", [1, D], F32, kind="ExternalInput").ap()
    wg_hbm = nc.dram_tensor("wg_t", [HID, FH], I32, kind="ExternalInput").ap()
    wu_hbm = nc.dram_tensor("wu_t", [HID, FH], I32, kind="ExternalInput").ap()
    wd_hbm = nc.dram_tensor("wd_t", [FH, HID], I32, kind="ExternalInput").ap()
    sg_hbm = nc.dram_tensor("s_gate", [1, FH], F32, kind="ExternalInput").ap()
    su_hbm = nc.dram_tensor("s_up", [1, FH], F32, kind="ExternalInput").ap()
    sd_hbm = nc.dram_tensor("s_down", [1, HID], F32, kind="ExternalInput").ap()
    tri_hbm = nc.dram_tensor("tri", [128, 128], F32, kind="ExternalInput").ap()
    iot_hbm = nc.dram_tensor("iot64", [1, 64], F32, kind="ExternalInput").ap()
    tcol_hbm = nc.dram_tensor("tcol_h", [128, NTT], F32, kind="ExternalInput").ap()
    id_hbm = nc.dram_tensor("ident", [128, 128], BF16, kind="ExternalInput").ap()

    out_hbm = nc.dram_tensor("out", [OWN, HID], F32, kind="ExternalOutput").ap()

    rg = [[0, 1, 2, 3], [4, 5, 6, 7]]
    cc_space = maybe_share_collective_output_space("ReduceScatter", rg)
    ccin_hbm = nc.dram_tensor("cc_in", [S, HID], F32, kind="Internal").ap()
    ccout_hbm = nc.dram_tensor("cc_out", [OWN, HID], F32, kind="Internal",
                               addr_space=cc_space).ap()

    ident = b.sbuf("ident_sb", [128, 128], BF16)
    tri = b.sbuf("tri_sb", [128, 128], F32)
    qg_rep = b.sbuf("qg_rep", [128, D], F32)
    qb_rep = b.sbuf("qb_rep", [128, D], F32)
    kg_rep = b.sbuf("kg_rep", [128, D], F32)
    kb_rep = b.sbuf("kb_rep", [128, D], F32)
    cst = b.sbuf("cst", [128, 8], F32)
    invfreq = b.sbuf("invfreq", [128, 64], F32)
    invfreq_i = b.sbuf("invfreq_i", [128, 64], F32)

    ARENA_KB = 200
    arena = b.sbuf("arena", [128, ARENA_KB * 1024], mybir.dt.uint8)
    arena_base = nc.lookup_mloc(arena).addr

    def at(off, name, shape, dtype):
        per_part = int(np.prod(shape[1:])) * mybir.dt.size(dtype)
        off = (off + 31) // 32 * 32
        assert off + per_part <= ARENA_KB * 1024, (name, off, per_part)
        t = nc.alloc_sbuf_tensor_at(name, shape, dtype, offset=arena_base + off)
        return t, off + per_part

    o = 0
    q_fm, o = at(o, "q_fm", [128, NH, NTT, 128], BF16)
    k_fm, o = at(o, "k_fm", [128, NTT * 128], BF16)
    v_tm, o = at(o, "v_tm", [128, NTT, 128], BF16)
    attn_fm, o = at(o, "attn_fm", [128, NH, NTT, 128], BF16)
    wqkv_sb, o = at(o, "wqkv_sb", [128, NDC, 768], BF16)
    wo_sb, o = at(o, "wo_sb", [128, 4, HID], BF16)
    AB_END = o

    o = AB_END
    x_sb, o = at(o, "x_sb", [128, 2, HID], F32)
    xn_tm, o = at(o, "xn_tm", [128, 2, HID], BF16)
    xn_fm, o = at(o, "xn_fm", [128, 2, NDC, 128], BF16)
    cos_tab, o = at(o, "cos_tab", [128, NTT, D], BF16)
    sin_tab, o = at(o, "sin_tab", [128, NTT, D], BF16)
    tcol, o = at(o, "tcol", [128, NTT], F32)
    sqd, o = at(o, "sqd", [128, 2, 640], F32)
    st_a, o = at(o, "st_a", [128, 2, 40], F32)
    xc_s, o = at(o, "xc_s", [128, 2, 640], F32)
    y_s, o = at(o, "y_s", [128, 2, 640], F32)
    r1_s, o = at(o, "r1_s", [128, 2, 640], F32)
    r2_s, o = at(o, "r2_s", [128, 2, 640], F32)
    qk_tm, o = at(o, "qk_tm", [128, 2, 640], BF16)
    sqx, o = at(o, "sqx", [128, HID], BF16)
    assert o <= ARENA_KB * 1024

    o = AB_END
    p_sb, o = at(o, "p_sb", [128, 2, 512], BF16)
    pt_sb, o = at(o, "pt_sb", [128, 2, 512], BF16)
    mcol, o = at(o, "mcol", [128, 4, 8], F32)
    lcol, o = at(o, "lcol", [128, 4, 8], F32)
    mrow, o = at(o, "mrow", [128, 4, 4], F32)
    atm, o = at(o, "atm", [128, 4, 128], BF16)
    po_sb, o = at(o, "po_sb", [128, 2, 512], F32)
    assert o <= ARENA_KB * 1024

    o = 0
    xn_own, o = at(o, "xn_own", [128, NDC, OWN], BF16)
    gu_fm, o = at(o, "gu_fm", [128, NHC, OWN], BF16)
    wgI, o = at(o, "wgI", [128, NDC, 128], I32)
    wuI, o = at(o, "wuI", [128, NDC, 128], I32)
    wg_sb, o = at(o, "wg_sb", [128, 2, NDC, 128], BF16)
    wu_sb, o = at(o, "wu_sb", [128, 2, NDC, 128], BF16)
    wd_sb, o = at(o, "wd_sb", [128, 2, 512], BF16)
    sd_rep, o = at(o, "sd_rep", [128, 512], F32)
    sg_fm, o = at(o, "sg_fm", [128, NHC], F32)
    su_fm, o = at(o, "su_fm", [128, NHC], F32)
    xo_sb, o = at(o, "xo_sb", [128, HID], F32)
    xno_tm, o = at(o, "xno_tm", [128, 2, HID], BF16)
    gs_tmp, o = at(o, "gs_tmp", [128, 2, 512], BF16)
    up_tmp, o = at(o, "up_tmp", [128, 2, 512], BF16)
    rs_sb, o = at(o, "rs_sb", [128, 2, 512], F32)
    xot_sb, o = at(o, "xot_sb", [128, 2, 512], F32)
    out_sb, o = at(o, "out_sb", [128, 2, 512], F32)
    st_d, o = at(o, "st_d", [128, 2, 8], F32)
    sqd_d, o = at(o, "sqd_d", [128, HID], BF16)
    assert o <= ARENA_KB * 1024

    pf = [b.psum(f"pf{i}", [128, 512], F32) for i in range(6)]
    pbank = [b.psum(f"pb{i}", [128, 1024], BF16) for i in range(2)]
    pbf = [pbank[i][:, :].bitcast(F32) for i in range(2)]
    pf4b = pf[4][:, :].bitcast(BF16)
    pf5b = pf[5][:, :].bitcast(BF16)

    for i, v in enumerate((C_EPS, ISD, C_PI2, C_NEG1, C_IVF)):
        nc.const_aps.register(F32, v, cst[:, i:i + 1])

    s_ld = b.sem("s_ld")
    s_iot = b.sem("s_iot")
    g_ld = b.sem("g_ld")
    a_x0 = b.sem("a_x0")
    a_x1 = b.sem("a_x1")
    bar = b.sem("bar")
    a_fin = b.sem("a_fin")
    a_v = b.sem("a_v")
    a_s = b.sem("a_s")
    a_g = b.sem("a_g")
    a_t = b.sem("a_t")
    b_fin = b.sem("b_fin")
    b_t = b.sem("b_t")
    b_v = b.sem("b_v")
    b_s = b.sem("b_s")
    c_t = b.sem("c_t")
    c_v = b.sem("c_v")
    c_s = b.sem("c_s")
    c_st0 = b.sem("c_st0")
    c_st1 = b.sem("c_st1")
    cc_done = b.sem("cc_done")
    d_sc = b.sem("d_sc")
    d_xo = b.sem("d_xo")
    d_wgu = b.sem("d_wgu")
    d_wgu2 = b.sem("d_wgu2")
    d_wd0 = b.sem("d_wd0")
    d_wd1 = b.sem("d_wd1")
    d_rs0 = b.sem("d_rs0")
    d_rs1 = b.sem("d_rs1")
    d_sd = b.sem("d_sd")
    d_v = b.sem("d_v")
    d_s = b.sem("d_s")
    d_t = b.sem("d_t")
    d_fin = b.sem("d_fin")
    o_st0 = b.sem("o_st0")
    o_st1 = b.sem("o_st1")

    def barrier():
        b.step("tensor", lambda t: t.ldweights(ident[:, :]), inc=bar, amt=1)
        b.step("vector", lambda v: v.memset(cst[:, 5:6], 0.0), inc=bar, amt=1)
        b.step("scalar", lambda s: s.activation(out=cst[:, 6:7], in_=cst[:, 6:7],
                                                func=AF.Copy), inc=bar, amt=1)
        b.step("gpsimd", lambda g: g.memset(cst[:, 7:8], 0.0), inc=bar, amt=1)
        b.step("sync", lambda s: s.nop(), inc=bar, amt=1)
        want = bar.n
        for e in ENGS:
            b.step(e, lambda x: x.nop(), waits=[(bar, want)])

    def ph0():
        def f_sync(s, sem):
            s.dma_start(out=tri[:, :], in_=tri_hbm).then_inc(sem, 16)
            s.dma_start(out=ident[:, :], in_=id_hbm).then_inc(sem, 16)
            s.dma_start(out=qg_rep[:, :], in_=qng_hbm.broadcast_to((128, D))).then_inc(sem, 16)
            s.dma_start(out=qb_rep[:, :], in_=qnb_hbm.broadcast_to((128, D))).then_inc(sem, 16)
            s.dma_start(out=kg_rep[:, :], in_=kng_hbm.broadcast_to((128, D))).then_inc(sem, 16)
            s.dma_start(out=kb_rep[:, :], in_=knb_hbm.broadcast_to((128, D))).then_inc(sem, 16)
        b.step("sync", f_sync, inc=s_ld, amt=6 * 16, self_inc=True)
        tok_small = (s_ld, s_ld.n)

        def f_gp(g, sem):
            src = wqkv_hbm.rearrange("(c p) f -> p c f", p=128)
            for i in range(4):
                g.dma_start(out=wqkv_sb[:, 4 * i:4 * i + 4, :],
                            in_=src[:, 4 * i:4 * i + 4, :]).then_inc(sem, 16)
            g.dma_start(out=wo_sb[:, :, :],
                        in_=wo_hbm.rearrange("(c p) f -> p c f", p=128)).then_inc(sem, 16)
        b.step("gpsimd", f_gp, inc=g_ld, amt=5 * 16, self_inc=True)
        tok_w = (g_ld, g_ld.n)

        def f_vec(v):
            inst = None
            for i, val in enumerate((C_EPS, ISD, C_PI2, C_NEG1, C_IVF, 0.0, 0.0, 0.0)):
                inst = v.memset(cst[:, i:i + 1], val)
            return inst
        b.step("vector", f_vec, inc=a_v)
        tok_cst = (a_v, a_v.n)

        def f_gp2(s, sem):
            s.dma_start(out=invfreq_i[:, :], in_=iot_hbm.broadcast_to((128, 64))
                        ).then_inc(sem, 16)
            s.dma_start(out=tcol[:, :], in_=tcol_hbm).then_inc(sem, 16)
        b.step("sync", f_gp2, inc=s_iot, amt=32, self_inc=True)
        tok_if = (s_iot, s_iot.n)

        def f_scl(s):
            return s.activation(out=invfreq[:, :], in_=invfreq_i[:, :], func=AF.Exp,
                                scale=cst[:, 4:5])
        b.step("scalar", f_scl, waits=[tok_if, tok_cst], inc=a_s)
        return tok_small, tok_w, (a_s, a_s.n)

    tok_small, tok_w, tok_if = ph0()

    def phaseA(ntiles=NTT, cut=99):
        t_tok = tok_if
        for tt in range(NTT):
            bf = tt % 2

            def f_vr(v, tt=tt, bf=bf):
                r2 = r2_s[:, bf, :]

                def reduce_angle(ang, o):
                    u = r2[:, o:o + 64]
                    ki = r2[:, o + 64:o + 128].bitcast(I32)
                    kf = r2[:, o + 128:o + 192]
                    v.tensor_scalar_mul(out=u, in0=ang, scalar1=1.0 / TWO_PI)
                    v.drain()
                    v.tensor_copy(out=ki, in_=u)
                    v.drain()
                    v.tensor_copy(out=kf, in_=ki)
                    phi = r2[:, o + 192:o + 256]
                    v.drain()
                    v.scalar_tensor_tensor(out=phi, in0=kf, scalar=-TWO_PI, in1=ang,
                                           op0=ALU.mult, op1=ALU.add)
                    m = r2[:, o:o + 64]
                    v.drain()
                    v.tensor_single_scalar(out=m, in_=phi, scalar=math.pi, op=ALU.is_gt)
                    v.drain()
                    v.scalar_tensor_tensor(out=phi, in0=m, scalar=-TWO_PI, in1=phi,
                                           op0=ALU.mult, op1=ALU.add)
                    v.drain()
                    v.tensor_single_scalar(out=m, in_=phi, scalar=-math.pi, op=ALU.is_lt)
                    v.drain()
                    return v.scalar_tensor_tensor(out=phi, in0=m, scalar=TWO_PI, in1=phi,
                                                  op0=ALU.mult, op1=ALU.add)
                th = r1_s[:, bf, 0:64]
                v.tensor_scalar(out=th, in0=invfreq[:, :], scalar1=tcol[:, tt:tt + 1],
                                scalar2=None, op0=ALU.mult)
                v.drain()
                reduce_angle(th, 0)
                ac = r1_s[:, bf, 64:128]
                v.drain()
                v.tensor_scalar_add(out=ac, in0=th, scalar1=C_PI2)
                v.drain()
                return reduce_angle(ac, 256)
            prev = (a_s, a_s.n) if tt >= 2 else None
            ang_tok = b.step("vector", f_vr,
                             waits=[t_tok, tok_if] + ([prev] if prev else []), inc=a_v)

            def f_sr(sc, tt=tt, bf=bf):
                phi = r2_s[:, bf, 192:256]
                psi = r2_s[:, bf, 448:512]
                sc.activation(out=sin_tab[:, tt, 0:64], in_=phi, func=AF.Sin,
                              scale=cst[:, 3:4])
                sc.activation(out=sin_tab[:, tt, 64:128], in_=phi, func=AF.Sin)
                cdup = cos_tab[:, tt, :].rearrange("p (r i) -> p r i", r=2)
                psb = psi.unsqueeze(1).broadcast_to((128, 2, 64))
                return sc.activation(out=cdup, in_=psb, func=AF.Sin)
            b.step("scalar", f_sr, waits=[ang_tok], inc=a_s)
        rope_done = (a_s, a_s.n)

        pb0_free = {0: None, 1: None}
        pb1_free = None
        for tt in range(ntiles):
            bf = tt % 2
            blanket = (a_fin, 3 * (tt - 1)) if tt >= 2 else None
            bl = [blanket] if blanket else []

            def f_x(s, tt=tt, bf=bf):
                src = x_hbm.rearrange("(n p) d -> p n d", p=128)
                return s.dma_start(out=x_sb[:, bf, :], in_=src[:, tt, :])
            x_tok = b.step("sync", f_x, waits=bl, inc=(a_x0 if bf == 0 else a_x1),
                           amt=16)

            last = {"tensor": None, "vector": None, "scalar": None, "gpsimd": None}
            if cut < 2:
                b.step("sync", lambda s, x_tok=x_tok: s.nop(), waits=[x_tok])
                for e in ("tensor", "vector", "scalar", "gpsimd"):
                    b.step(e, lambda x: x.nop(), waits=[x_tok], inc=a_fin)
                continue
            def f_v1(v, bf=bf):
                return v.reduce_sum(out=st_a[:, bf, 0:1], in_=x_sb[:, bf, :], axis=AX)
            sum_tok = b.step("vector", f_v1, waits=[x_tok], inc=a_v)

            def f_s1(s, bf=bf):
                return s.activation(out=sqx[:, :], in_=x_sb[:, bf, :], func=AF.Square,
                                    accum_out=st_a[:, bf, 1:2])
            ssq_tok = b.step("scalar", f_s1, waits=[x_tok], inc=a_s)

            def f_v2(v, bf=bf):
                stx = st_a[:, bf, :]
                v.tensor_scalar_mul(out=stx[:, 2:3], in0=stx[:, 0:1], scalar1=1.0 / HID)
                v.drain()
                v.tensor_tensor(out=stx[:, 3:4], in0=stx[:, 2:3], in1=stx[:, 2:3],
                                op=ALU.mult)
                v.drain()
                return v.scalar_tensor_tensor(out=stx[:, 4:5], in0=stx[:, 1:2],
                                              scalar=1.0 / HID, in1=stx[:, 3:4],
                                              op0=ALU.mult, op1=ALU.subtract)
            var_tok = b.step("vector", f_v2, waits=[ssq_tok, sum_tok], inc=a_v)
            if cut < 3:
                b.step("tensor", lambda t: t.nop(), waits=[var_tok], inc=a_fin)
                b.step("vector", lambda v: v.nop(), waits=[var_tok], inc=a_fin)
                b.step("scalar", lambda s: s.nop(), waits=[ssq_tok], inc=a_fin)
                b.step("gpsimd", lambda g: g.nop(), waits=[var_tok], inc=a_fin)
                continue

            def f_s2(s, bf=bf):
                return s.activation(out=st_a[:, bf, 5:6], in_=st_a[:, bf, 4:5],
                                    func=AF.Sqrt, bias=cst[:, 0:1])
            std_tok = b.step("scalar", f_s2, waits=[var_tok], inc=a_s)

            def f_v3(v, bf=bf):
                v.reciprocal(out=st_a[:, bf, 6:7], in_=st_a[:, bf, 5:6])
                v.drain()
                return v.tensor_scalar(out=st_a[:, bf, 7:8], in0=st_a[:, bf, 2:3],
                                       scalar1=st_a[:, bf, 6:7], scalar2=-1.0,
                                       op0=ALU.mult, op1=ALU.mult)
            inv_tok = b.step("vector", f_v3, waits=[std_tok], inc=a_v)

            def f_g1(sc, bf=bf):
                return sc.activation(out=xn_tm[:, bf, :], in_=x_sb[:, bf, :],
                                     func=AF.Identity, scale=st_a[:, bf, 6:7],
                                     bias=st_a[:, bf, 7:8])
            xn_tok = b.step("scalar", f_g1, waits=[inv_tok], inc=a_s)
            if cut < 5:
                b.step("tensor", lambda t: t.nop(), waits=[xn_tok], inc=a_fin)
                b.step("vector", lambda v: v.nop(), waits=[xn_tok], inc=a_fin)
                b.step("scalar", lambda s: s.nop(), waits=[std_tok], inc=a_fin)
                b.step("gpsimd", lambda g: g.nop(), waits=[xn_tok], inc=a_fin)
                continue

            grp_toks = []
            for grp in range(4):
                half = grp % 2

                def f_tr(t, bf=bf, grp=grp, half=half):
                    for j in range(4):
                        dc = grp * 4 + j
                        inst = t.transpose(out=pbank[half][:, 128 * j:128 * j + 128],
                                           in_=xn_tm[:, bf, 128 * dc:128 * dc + 128],
                                           identity=ident[:, :])
                    return inst
                w = [xn_tok, tok_small]
                if pb0_free[half] is not None:
                    w.append(pb0_free[half])
                tr_tok = b.step("tensor", f_tr, waits=w, inc=a_t)

                def f_ev(v, bf=bf, grp=grp, half=half):
                    return v.tensor_copy(
                        out=xn_fm[:, bf, grp * 4:(grp + 1) * 4, :],
                        in_=pbank[half][:, 0:512].rearrange("p (c i) -> p c i", c=4))
                ev_tok = b.step("vector", f_ev, waits=[tr_tok], inc=a_v)
                pb0_free[half] = ev_tok
                grp_toks.append(ev_tok)

            bq = pf[0 + bf]
            bkv = pf[2 + bf]

            def f_mm(t, bf=bf, bq=bq, bkv=bkv):
                for dc in range(NDC):
                    lhs = xn_fm[:, bf, dc, :]
                    t.matmul(out=bq[:, :], lhsT=lhs, rhs=wqkv_sb[:, dc, 0:512],
                             start=(dc == 0), stop=(dc == NDC - 1))
                    inst = t.matmul(out=bkv[:, 0:256], lhsT=lhs,
                                    rhs=wqkv_sb[:, dc, 512:768],
                                    start=(dc == 0), stop=(dc == NDC - 1))
                return inst
            if cut < 6:
                b.step("tensor", lambda t, w=grp_toks[-1]: t.nop(), waits=[grp_toks[-1]], inc=a_fin)
                b.step("vector", lambda v: v.nop(), waits=[grp_toks[-1]], inc=a_fin)
                b.step("scalar", lambda s: s.nop(), waits=[std_tok], inc=a_fin)
                b.step("gpsimd", lambda g: g.nop(), waits=[xn_tok], inc=a_fin)
                continue
            mm_tok = b.step("tensor", f_mm, waits=[grp_toks[-1], tok_w], inc=a_t)

            if cut < 7:
                b.step("tensor", lambda t: t.nop(), waits=[mm_tok], inc=a_fin)
                b.step("vector", lambda v: v.nop(), waits=[mm_tok], inc=a_fin)
                b.step("scalar", lambda s: s.nop(), waits=[std_tok], inc=a_fin)
                b.step("gpsimd", lambda g: g.nop(), waits=[xn_tok], inc=a_fin)
                continue
            def f_v5(v, bf=bf, tt=tt, bq=bq, bkv=bkv):
                v.tensor_copy(out=v_tm[:, tt, :], in_=bkv[:, 128:256])
                stx = st_a[:, bf, :]
                v.reduce_sum(out=stx[:, 8:12],
                             in_=bq[:, :].rearrange("p (h d) -> p h d", h=4), axis=AX)
                return v.reduce_sum(out=stx[:, 12:13], in_=bkv[:, 0:128], axis=AX)
            qsum_tok = b.step("vector", f_v5, waits=[mm_tok], inc=a_v)
            if cut < 62:
                b.step("tensor", lambda t: t.nop(), waits=[qsum_tok], inc=a_fin)
                b.step("vector", lambda v: v.nop(), waits=[qsum_tok], inc=a_fin)
                b.step("scalar", lambda s: s.nop(), waits=[std_tok], inc=a_fin)
                b.step("gpsimd", lambda g: g.nop(), waits=[xn_tok], inc=a_fin)
                continue

            def f_s3(s, bf=bf, bq=bq, bkv=bkv):
                s.activation(out=sqd[:, bf, 0:512], in_=bq[:, :], func=AF.Square)
                return s.activation(out=sqd[:, bf, 512:640], in_=bkv[:, 0:128],
                                    func=AF.Square)
            sq_tok = b.step("scalar", f_s3, waits=[mm_tok, qsum_tok], inc=a_s)
            if cut < 63:
                b.step("tensor", lambda t: t.nop(), waits=[sq_tok], inc=a_fin)
                b.step("vector", lambda v: v.nop(), waits=[qsum_tok], inc=a_fin)
                b.step("scalar", lambda s: s.nop(), waits=[sq_tok], inc=a_fin)
                b.step("gpsimd", lambda g: g.nop(), waits=[xn_tok], inc=a_fin)
                continue

            def f_v6(v, bf=bf):
                stx = st_a[:, bf, :]
                v.reduce_sum(out=stx[:, 13:18],
                             in_=sqd[:, bf, :].rearrange("p (h d) -> p h d", h=5),
                             axis=AX)
                v.tensor_scalar_mul(out=stx[:, 18:23], in0=stx[:, 8:13], scalar1=1.0 / D)
                v.drain()
                v.tensor_tensor(out=stx[:, 23:28], in0=stx[:, 18:23], in1=stx[:, 18:23],
                                op=ALU.mult)
                v.drain()
                return v.scalar_tensor_tensor(out=stx[:, 28:33], in0=stx[:, 13:18],
                                              scalar=1.0 / D, in1=stx[:, 23:28],
                                              op0=ALU.mult, op1=ALU.subtract)
            qvar_tok = b.step("vector", f_v6, waits=[sq_tok, qsum_tok], inc=a_v)
            if cut < 64:
                b.step("tensor", lambda t: t.nop(), waits=[qvar_tok], inc=a_fin)
                b.step("vector", lambda v: v.nop(), waits=[qvar_tok], inc=a_fin)
                b.step("scalar", lambda s: s.nop(), waits=[sq_tok], inc=a_fin)
                b.step("gpsimd", lambda g: g.nop(), waits=[xn_tok], inc=a_fin)
                continue

            def f_s4(s, bf=bf):
                return s.activation(out=st_a[:, bf, 33:38], in_=st_a[:, bf, 28:33],
                                    func=AF.Sqrt, bias=cst[:, 0:1])
            qstd_tok = b.step("scalar", f_s4, waits=[qvar_tok], inc=a_s)
            if cut < 65:
                b.step("tensor", lambda t: t.nop(), waits=[qvar_tok], inc=a_fin)
                b.step("vector", lambda v: v.nop(), waits=[qvar_tok], inc=a_fin)
                b.step("scalar", lambda s: s.nop(), waits=[qstd_tok], inc=a_fin)
                b.step("gpsimd", lambda g: g.nop(), waits=[xn_tok], inc=a_fin)
                continue

            def f_v7(v, bf=bf, bq=bq, bkv=bkv):
                stx = st_a[:, bf, :]
                v.reciprocal(out=stx[:, 23:28], in_=stx[:, 33:38])
                v.drain()
                inst = None
                for seg in range(5):
                    src = bq[:, 128 * seg:128 * seg + 128] if seg < 4 else bkv[:, 0:128]
                    inst = v.tensor_scalar(out=xc_s[:, bf, 128 * seg:128 * seg + 128],
                                           in0=src,
                                           scalar1=stx[:, 18 + seg:19 + seg],
                                           scalar2=stx[:, 23 + seg:24 + seg],
                                           op0=ALU.subtract, op1=ALU.mult)
                return inst
            xc_tok = b.step("vector", f_v7, waits=[qstd_tok], inc=a_v)
            if cut < 8:
                b.step("tensor", lambda t: t.nop(), waits=[xc_tok], inc=a_fin)
                b.step("vector", lambda v: v.nop(), waits=[xc_tok], inc=a_fin)
                b.step("scalar", lambda s: s.nop(), waits=[qstd_tok], inc=a_fin)
                b.step("gpsimd", lambda g: g.nop(), waits=[xc_tok], inc=a_fin)
                continue

            def f_g2(v, bf=bf):
                gq = qg_rep[:, :].unsqueeze(1).broadcast_to((128, 4, D))
                bqr = qb_rep[:, :].unsqueeze(1).broadcast_to((128, 4, D))
                xq = xc_s[:, bf, 0:512].rearrange("p (h d) -> p h d", h=4)
                yq = y_s[:, bf, 0:512].rearrange("p (h d) -> p h d", h=4)
                v.tensor_tensor(out=yq, in0=xq, in1=gq, op=ALU.mult)
                v.tensor_tensor(out=y_s[:, bf, 512:640], in0=xc_s[:, bf, 512:640],
                                in1=kg_rep[:, :], op=ALU.mult)
                v.drain()
                v.tensor_tensor(out=yq, in0=yq, in1=bqr, op=ALU.add)
                return v.tensor_tensor(out=y_s[:, bf, 512:640], in0=y_s[:, bf, 512:640],
                                       in1=kb_rep[:, :], op=ALU.add)
            y_tok = b.step("vector", f_g2, waits=[xc_tok, tok_small], inc=a_v)

            def f_v8(v, bf=bf, tt=tt):
                cos5 = cos_tab[:, tt, :].unsqueeze(1).broadcast_to((128, 5, D))
                sin5 = sin_tab[:, tt, :].unsqueeze(1).broadcast_to((128, 5, D))
                yv = y_s[:, bf, :].rearrange("p (s d) -> p s d", s=5)
                yrot = y_s[:, bf, :].rearrange("p (s j i) -> p s j i", s=5, j=2)[:, :, ::-1, :]
                t1 = r1_s[:, bf, :].rearrange("p (s d) -> p s d", s=5)
                t2 = r2_s[:, bf, :].rearrange("p (s j i) -> p s j i", s=5, j=2)
                sin4 = sin5.rearrange("p s (j i) -> p s j i", j=2)
                v.tensor_tensor(out=t1, in0=yv, in1=cos5, op=ALU.mult)
                v.tensor_tensor(out=t2, in0=yrot, in1=sin4, op=ALU.mult)
                v.drain()
                return v.tensor_tensor(out=qk_tm[:, bf, :].rearrange("p (s d) -> p s d", s=5),
                                       in0=t1, in1=t2, op=ALU.add)
            qk_tok = b.step("vector", f_v8, waits=[y_tok, rope_done], inc=a_v)
            if cut < 10:
                b.step("tensor", lambda t: t.nop(), waits=[qk_tok], inc=a_fin)
                b.step("vector", lambda v: v.nop(), waits=[qk_tok], inc=a_fin)
                b.step("scalar", lambda s: s.nop(), waits=[qstd_tok], inc=a_fin)
                continue

            def f_t3(t, bf=bf):
                for seg in range(5):
                    inst = t.transpose(out=pf4b[:, 128 * seg:128 * seg + 128],
                                       in_=qk_tm[:, bf, 128 * seg:128 * seg + 128],
                                       identity=ident[:, :])
                return inst
            w = [qk_tok]
            if pb1_free is not None:
                w.append(pb1_free)
            qkt_tok = b.step("tensor", f_t3, waits=w, inc=a_t)

            def f_v9(v, tt=tt):
                v.tensor_copy(out=q_fm[:, 0:4, tt, :],
                              in_=pf4b[:, 0:512].rearrange("p (h i) -> p h i", h=4))
                return v.tensor_copy(out=k_fm[:, 128 * tt:128 * tt + 128],
                                     in_=pf4b[:, 512:640])
            fm_tok = b.step("vector", f_v9, waits=[qkt_tok], inc=a_v)
            pb1_free = fm_tok

            b.step("tensor", lambda t: t.nop(), waits=[qkt_tok], inc=a_fin)
            b.step("vector", lambda v: v.nop(), waits=[fm_tok], inc=a_fin)
            b.step("scalar", lambda s: s.nop(), waits=[qstd_tok, sq_tok], inc=a_fin)
            
    if stage.startswith("An:"):
        phaseA(int(stage.split(":")[1]), cut)
        barrier()
        _dbg_dump(b, [("d_cos", cos_tab[:, :, :], [128, NTT, 128], BF16),
                      ("d_sin", sin_tab[:, :, :], [128, NTT, 128], BF16),
                      ("d_if", invfreq[:, :], [128, 64], F32),
                      ("d_tcol", tcol[:, :], [128, NTT], F32),
                      ("d_r2a", r2_s[:, 0, :], [128, 640], F32),
                      ("d_r2b", r2_s[:, 1, :], [128, 640], F32),
                      ("d_r1a", r1_s[:, 0, :], [128, 640], F32),
                      ("d_r1b", r1_s[:, 1, :], [128, 640], F32)])
        return _finish(b, nc)
    phaseA()
    if stage == "A":
        barrier()
        _dbg_dump(b, [("d_qfm", q_fm[:, 0, :, :], [128, NTT * 128], BF16),
                      ("d_qfm3", q_fm[:, 3, :, :], [128, NTT * 128], BF16),
                      ("d_kfm", k_fm[:, :], [128, NTT * 128], BF16),
                      ("d_vtm", v_tm[:, :, :], [128, NTT * 128], BF16),
                      ("d_cos", cos_tab[:, :, :], [128, NTT * 128], BF16),
                      ("d_sin", sin_tab[:, :, :], [128, NTT * 128], BF16)])
        return _finish(b, nc)
    barrier()

    def phaseB(nunits=4 * NTT, bcut=9):
        bar_tok = (bar, bar.n)
        s1_free = {0: bar_tok, 1: bar_tok}
        s2_free = {0: bar_tok, 1: bar_tok}
        ptb_free = {0: bar_tok, 1: bar_tok}
        p_free = {0: bar_tok, 1: bar_tok}
        pt_free = {0: bar_tok, 1: bar_tok}
        ab_free = bar_tok
        atr_free = bar_tok

        for u in range(nunits):
            h, qb = u // NTT, u % NTT
            nkt = qb // 4 + 1
            kws = [512] * (nkt - 1) + [128 * (qb % 4) + 128]
            us = u % 4
            blanket = (b_fin, 3 * (u - 3)) if u >= 4 else None
            bl = [blanket] if blanket else []

            if bcut < 1:
                b.step("tensor", lambda t: t.nop(), waits=bl, inc=b_fin)
                b.step("vector", lambda v: v.nop(), waits=bl, inc=b_fin)
                b.step("scalar", lambda s: s.nop(), waits=bl, inc=b_fin)
                continue
            s1_toks = []
            mx_toks = []
            for kt in range(nkt):
                kw = kws[kt]
                diag = kt == nkt - 1

                def f_s1mm(t, h=h, qb=qb, kt=kt, kw=kw):
                    return t.matmul(out=pf[kt % 2][:, 0:kw], lhsT=q_fm[:, h, qb, :],
                                    rhs=k_fm[:, 512 * kt:512 * kt + kw],
                                    start=True, stop=True)
                tok = b.step("tensor", f_s1mm, waits=bl + [s1_free[kt % 2]], inc=b_t)
                s1_toks.append(tok)

                def f_mx(v, kt=kt, kw=kw, diag=diag, us=us):
                    pk = pf[kt % 2]
                    if diag:
                        v.tensor_tensor(out=pk[:, kw - 128:kw], in0=pk[:, kw - 128:kw],
                                        in1=tri[:, :], op=ALU.add)
                        v.drain()
                    return v.reduce_max(out=mcol[:, us, kt:kt + 1], in_=pk[:, 0:kw],
                                        axis=AX)
                tok = b.step("vector", f_mx, waits=bl + [s1_toks[kt]], inc=b_v)
                s1_free[kt % 2] = tok
                mx_toks.append(tok)

            def f_m(v, us=us, nkt=nkt):
                v.reduce_max(out=mrow[:, us, 2:3], in_=mcol[:, us, 0:nkt], axis=AX)
                v.drain()
                return v.tensor_scalar_mul(out=mrow[:, us, 0:1], in0=mrow[:, us, 2:3],
                                           scalar1=-ISD)
            m_tok = b.step("vector", f_m, waits=[mx_toks[-1]], inc=b_v)
            if bcut < 2:
                b.step("tensor", lambda t: t.nop(), waits=[s1_toks[-1]], inc=b_fin)
                b.step("vector", lambda v: v.nop(), waits=[m_tok], inc=b_fin)
                b.step("scalar", lambda s: s.nop(), waits=[(b_s, b_s.n)], inc=b_fin)
                continue

            v_last = None
            pend = []

            def emit_trv(kt, kw, ks, exp_tok, nch):
                nonlocal v_last
                if bcut < 4:
                    p_free[ks] = exp_tok
                    return

                def f_tr(t, kw=kw, ks=ks, nch=nch):
                    for ci in range(nch):
                        inst = t.transpose(out=pbank[ks][:, 128 * ci:128 * ci + 128],
                                           in_=p_sb[:, ks, 128 * ci:128 * ci + 128],
                                           identity=ident[:, :])
                    return inst
                tr_tok = b.step("tensor", f_tr, waits=[exp_tok, ptb_free[ks]], inc=b_t)
                p_free[ks] = tr_tok

                if kt % 2 == 0:
                    def f_ev(v, kw=kw, ks=ks):
                        return v.tensor_copy(out=pt_sb[:, ks, 0:kw],
                                             in_=pbank[ks][:, 0:kw])
                    ev_tok = b.step("vector", f_ev, waits=[tr_tok, pt_free[ks]], inc=b_v)
                else:
                    def f_ev(sc, kw=kw, ks=ks):
                        return sc.activation(out=pt_sb[:, ks, 0:kw],
                                             in_=pbank[ks][:, 0:kw],
                                             func=AF.Copy)
                    ev_tok = b.step("scalar", f_ev, waits=[tr_tok, pt_free[ks]], inc=b_s)
                ptb_free[ks] = ev_tok

                if bcut < 5:
                    pt_free[ks] = ev_tok
                    return
                vw = [ev_tok]
                if kt == 0:
                    vw.append(ab_free)

                def f_vmm(t, kt=kt, ks=ks, nch=nch, nkt=nkt):
                    for ci in range(nch):
                        inst = t.matmul(out=pf[4][:, 0:128],
                                        lhsT=pt_sb[:, ks, 128 * ci:128 * ci + 128],
                                        rhs=v_tm[:, 4 * kt + ci, :],
                                        start=(kt == 0 and ci == 0),
                                        stop=(kt == nkt - 1 and ci == nch - 1))
                    return inst
                v_tok = b.step("tensor", f_vmm, waits=vw, inc=b_t)
                pt_free[ks] = v_tok
                v_last = v_tok

            for kt in range(nkt):
                kw = kws[kt]
                diag = kt == nkt - 1
                ks = kt % 2

                def f_s2mm(t, h=h, qb=qb, kt=kt, kw=kw, ks=ks):
                    return t.matmul(out=pf[2 + ks][:, 0:kw], lhsT=q_fm[:, h, qb, :],
                                    rhs=k_fm[:, 512 * kt:512 * kt + kw],
                                    start=True, stop=True)
                s2_tok = b.step("tensor", f_s2mm, waits=bl + [s2_free[ks]], inc=b_t)

                exp_w = [s2_tok, m_tok, p_free[ks]]
                if diag:
                    def f_mk2(v, kw=kw, ks=ks):
                        pk = pf[2 + ks]
                        return v.tensor_tensor(out=pk[:, kw - 128:kw],
                                               in0=pk[:, kw - 128:kw],
                                               in1=tri[:, :], op=ALU.add)
                    mk2_tok = b.step("vector", f_mk2, waits=[s2_tok], inc=b_v)
                    exp_w.append(mk2_tok)

                def f_exp(sc, kt=kt, kw=kw, ks=ks, us=us):
                    return sc.activation(out=p_sb[:, ks, 0:kw], in_=pf[2 + ks][:, 0:kw],
                                         func=AF.Exp, scale=cst[:, 1:2],
                                         bias=mrow[:, us, 0:1],
                                         accum_out=lcol[:, us, kt:kt + 1])
                exp_tok = b.step("scalar", f_exp, waits=exp_w, inc=b_s)
                s2_free[ks] = exp_tok

                pend.append((kt, kw, ks, exp_tok, kw // 128))
                if len(pend) >= 2:
                    emit_trv(*pend.pop(0))
            while pend:
                emit_trv(*pend.pop(0))

            if bcut < 6:
                b.step("tensor", lambda t: t.nop(), waits=[(b_t, b_t.n)], inc=b_fin)
                b.step("vector", lambda v: v.nop(), waits=[(b_v, b_v.n)], inc=b_fin)
                b.step("scalar", lambda s: s.nop(), waits=[(b_s, b_s.n)], inc=b_fin)
                continue
            def f_l(v, us=us, nkt=nkt):
                v.reduce_sum(out=mrow[:, us, 3:4], in_=lcol[:, us, 0:nkt], axis=AX)
                v.drain()
                return v.reciprocal(out=mrow[:, us, 1:2], in_=mrow[:, us, 3:4])
            l_tok = b.step("vector", f_l, waits=[(b_s, b_s.n)], inc=b_v)

            def f_ae(v, us=us):
                return v.tensor_scalar_mul(out=atm[:, us, :], in0=pf[4][:, 0:128],
                                           scalar1=mrow[:, us, 1:2])
            ae_tok = b.step("vector", f_ae, waits=[l_tok, v_last], inc=b_v)
            ab_free = ae_tok

            def f_at(t, us=us):
                return t.transpose(out=pf5b[:, 0:128], in_=atm[:, us, :],
                                   identity=ident[:, :])
            at_tok = b.step("tensor", f_at, waits=[ae_tok, atr_free], inc=b_t)

            def f_af(v, h=h, qb=qb):
                return v.tensor_copy(out=attn_fm[:, h, qb, :], in_=pf5b[:, 0:128])
            af_tok = b.step("vector", f_af, waits=[at_tok], inc=b_v)
            atr_free = af_tok

            b.step("tensor", lambda t: t.nop(), waits=[at_tok], inc=b_fin)
            b.step("vector", lambda v: v.nop(), waits=[af_tok], inc=b_fin)
            b.step("scalar", lambda s: s.nop(), waits=[(b_s, b_s.n)], inc=b_fin)

    if stage.startswith("Bn:"):
        parts = stage.split(":")
        phaseB(int(parts[1]), int(parts[2]) if len(parts) > 2 else 9)
        barrier()
        _dbg_dump(b, [("d_attn", attn_fm[:, 0, :, :], [128, NTT * 128], BF16)])
        return _finish(b, nc)
    phaseB()
    if stage == "B":
        barrier()
        _dbg_dump(b, [("d_attn", attn_fm[:, 0, :, :], [128, NTT * 128], BF16),
                      ("d_attn3", attn_fm[:, 3, :, :], [128, NTT * 128], BF16)])
        return _finish(b, nc)
    barrier()

    def phaseB2():
        bar_tok = (bar, bar.n)
        bank_free = {0: bar_tok, 1: bar_tok}
        po_free = {0: bar_tok, 1: bar_tok}
        for i, (os_, tt) in enumerate([(a, t) for a in range(4) for t in range(NTT)]):
            sl = i % 2

            def f_mm(t, os_=os_, tt=tt, sl=sl):
                for fc in range(4):
                    inst = t.matmul(out=pf[sl][:, :], lhsT=attn_fm[:, fc, tt, :],
                                    rhs=wo_sb[:, fc, 512 * os_:512 * os_ + 512],
                                    start=(fc == 0), stop=(fc == 3))
                return inst
            mm_tok = b.step("tensor", f_mm, waits=[bank_free[sl]], inc=c_t)

            if i % 2 == 0:
                def f_ev(v, sl=sl):
                    return v.tensor_copy(out=po_sb[:, sl, :], in_=pf[sl][:, :])
                ev_tok = b.step("vector", f_ev, waits=[mm_tok, po_free[sl]], inc=c_v)
            else:
                def f_ev(s, sl=sl):
                    return s.activation(out=po_sb[:, sl, :], in_=pf[sl][:, :],
                                        func=AF.Copy)
                ev_tok = b.step("scalar", f_ev, waits=[mm_tok, po_free[sl]], inc=c_s)
            bank_free[sl] = ev_tok

            def f_st(s, os_=os_, tt=tt, sl=sl):
                return s.dma_start(out=ccin_hbm[128 * tt:128 * tt + 128,
                                                512 * os_:512 * os_ + 512],
                                   in_=po_sb[:, sl, :])
            st_tok = b.step("sync", f_st, waits=[ev_tok],
                            inc=(c_st0 if sl == 0 else c_st1), amt=16)
            po_free[sl] = st_tok

        def f_cc(g):
            return g.collective_compute(
                "ReduceScatter", ALU.add, replica_groups=rg,
                ins=[ccin_hbm[:, :]], outs=[ccout_hbm[:, :]])
        b.step("gpsimd", f_cc, waits=[(c_st0, c_st0.n), (c_st1, c_st1.n)],
               inc=cc_done)

    phaseB2()
    barrier()
    cc_tok = (cc_done, cc_done.n)

    def phaseD(d0_only=False):
        bar_tok = (bar, bar.n)

        def f_sc(s, sem):
            s.dma_start(out=sg_fm[:, :],
                        in_=sg_hbm.rearrange("o (c p) -> p (o c)", p=128),
                        allow_slow_non_contiguous=True).then_inc(sem, 16)
            s.dma_start(out=su_fm[:, :],
                        in_=su_hbm.rearrange("o (c p) -> p (o c)", p=128),
                        allow_slow_non_contiguous=True).then_inc(sem, 16)
        b.step("sync", f_sc, inc=d_sc, amt=2 * 16, self_inc=True)
        sc_tok = (d_sc, d_sc.n)

        pb0_free = {0: bar_tok, 1: bar_tok}
        x_free = bar_tok
        xn_toks = []
        for ot in range(8):
            bf = ot % 2

            def f_x(s, ot=ot):
                src = xown_hbm.rearrange("(n p) d -> p n d", p=128)
                return s.dma_start(out=xo_sb[:, :], in_=src[:, ot, :])
            x_tok = b.step("sync", f_x, waits=[x_free], inc=d_xo, amt=16)

            def f_v1(v, bf=bf):
                return v.reduce_sum(out=st_d[:, bf, 0:1], in_=xo_sb[:, :], axis=AX)
            sum_tok = b.step("vector", f_v1, waits=[x_tok], inc=d_v)

            def f_s1(s, bf=bf):
                return s.activation(out=sqd_d[:, :], in_=xo_sb[:, :], func=AF.Square,
                                    accum_out=st_d[:, bf, 1:2])
            ssq_tok = b.step("scalar", f_s1, waits=[x_tok], inc=d_s)

            def f_v2(v, bf=bf):
                stx = st_d[:, bf, :]
                v.tensor_scalar_mul(out=stx[:, 2:3], in0=stx[:, 0:1], scalar1=1.0 / HID)
                v.drain()
                v.tensor_tensor(out=stx[:, 3:4], in0=stx[:, 2:3], in1=stx[:, 2:3],
                                op=ALU.mult)
                v.drain()
                return v.scalar_tensor_tensor(out=stx[:, 4:5], in0=stx[:, 1:2],
                                              scalar=1.0 / HID, in1=stx[:, 3:4],
                                              op0=ALU.mult, op1=ALU.subtract)
            var_tok = b.step("vector", f_v2, waits=[ssq_tok, sum_tok], inc=d_v)

            def f_s2(s, bf=bf):
                return s.activation(out=st_d[:, bf, 5:6], in_=st_d[:, bf, 4:5],
                                    func=AF.Sqrt, bias=cst[:, 0:1])
            std_tok = b.step("scalar", f_s2, waits=[var_tok], inc=d_s)

            def f_v3(v, bf=bf):
                v.reciprocal(out=st_d[:, bf, 6:7], in_=st_d[:, bf, 5:6])
                v.drain()
                return v.tensor_scalar(out=xno_tm[:, bf, :], in0=xo_sb[:, :],
                                       scalar1=st_d[:, bf, 2:3],
                                       scalar2=st_d[:, bf, 6:7],
                                       op0=ALU.subtract, op1=ALU.mult)
            xn_tok = b.step("vector", f_v3, waits=[std_tok, ssq_tok], inc=d_v)
            x_free = xn_tok

            for grp in range(4):
                half = grp % 2

                def f_tr(t, bf=bf, grp=grp, half=half):
                    for j in range(4):
                        dc = grp * 4 + j
                        inst = t.transpose(out=pbank[half][:, 128 * j:128 * j + 128],
                                           in_=xno_tm[:, bf, 128 * dc:128 * dc + 128],
                                           identity=ident[:, :])
                    return inst
                tr_tok = b.step("tensor", f_tr, waits=[xn_tok, pb0_free[half]], inc=d_t)

                def f_ev(v, ot=ot, grp=grp, half=half):
                    return v.tensor_copy(
                        out=xn_own[:, grp * 4:(grp + 1) * 4, 128 * ot:128 * ot + 128],
                        in_=pbank[half][:, 0:512].rearrange("p (c i) -> p c i", c=4))
                ev_tok = b.step("vector", f_ev, waits=[tr_tok], inc=d_v)
                pb0_free[half] = ev_tok
                xn_toks.append(ev_tok)
        xn_done = xn_toks[-1]
        barrier()
        xn_done = (bar, bar.n)
        if d0_only:
            return

        wld_free = {0: [bar_tok], 1: [bar_tok]}
        wcast_free = {0: [bar_tok], 1: [bar_tok]}
        gs_free = {0: bar_tok, 1: bar_tok}
        up_free = {0: bar_tok, 1: bar_tok}
        bank_free = {id(pbf[0]): pb0_free[0], id(pbf[1]): pb0_free[1]}
        gu_last = None
        for hc in range(NHC):
            sl = hc % 2

            def f_wl(s, hc=hc):
                return s.dma_start(out=wgI[:, :, :],
                                   in_=wg_hbm.rearrange("(c p) f -> p c f", p=128
                                                        )[:, :, 128 * hc:128 * hc + 128])
            wl_tok = b.step("sync", f_wl, waits=wld_free[0], inc=d_wgu, amt=16)

            def f_wl2(sc, hc=hc):
                return sc.dma_start(out=wuI[:, :, :],
                                    in_=wu_hbm.rearrange("(c p) f -> p c f", p=128
                                                         )[:, :, 128 * hc:128 * hc + 128])
            wl2_tok = b.step("scalar", f_wl2, waits=wld_free[0], inc=d_wgu2, amt=16)

            def f_cg(v, sl=sl):
                return v.tensor_scalar_add(out=wg_sb[:, sl, :, :], in0=wgI[:, :, :],
                                           scalar1=-128.0)
            cg_tok = b.step("vector", f_cg, waits=[wl_tok] + wcast_free[sl], inc=d_v)

            def f_cu(s, sl=sl):
                return s.activation(out=wu_sb[:, sl, :, :], in_=wuI[:, :, :],
                                    func=AF.Copy, bias=-128.0)
            cu_tok = b.step("scalar", f_cu, waits=[wl2_tok] + wcast_free[sl], inc=d_s)
            wld_free[0] = [cg_tok, cu_tok]

            banks = [pf[0], pf[1], pf[2], pf[3]] if sl == 0 else \
                    [pf[4], pf[5], pbf[0], pbf[1]]
            for tsl in range(2):
                bg, bu = banks[tsl], banks[2 + tsl]

                def f_mm(t, sl=sl, tsl=tsl, bg=bg, bu=bu):
                    for dc in range(NDC):
                        t.matmul(out=bg[:, :], lhsT=wg_sb[:, sl, dc, :],
                                 rhs=xn_own[:, dc, 512 * tsl:512 * tsl + 512],
                                 start=(dc == 0), stop=(dc == NDC - 1))
                        inst = t.matmul(out=bu[:, :], lhsT=wu_sb[:, sl, dc, :],
                                        rhs=xn_own[:, dc, 512 * tsl:512 * tsl + 512],
                                        start=(dc == 0), stop=(dc == NDC - 1))
                    return inst
                mmw = [cg_tok, cu_tok, xn_done]
                for bk in (id(bg), id(bu)):
                    if bk in bank_free:
                        mmw.append(bank_free[bk])
                mm_tok = b.step("tensor", f_mm, waits=mmw, inc=d_t)

                def f_sil(s, hc=hc, tsl=tsl, bg=bg):
                    return s.activation(out=gs_tmp[:, tsl, :], in_=bg[:, :],
                                        func=AF.Silu, scale=sg_fm[:, hc:hc + 1])
                sil_tok = b.step("scalar", f_sil,
                                 waits=[mm_tok, sc_tok, gs_free[tsl]], inc=d_s)

                def f_up(v, hc=hc, tsl=tsl, bu=bu):
                    return v.tensor_scalar_mul(out=up_tmp[:, tsl, :], in0=bu[:, :],
                                               scalar1=su_fm[:, hc:hc + 1])
                up_tok = b.step("vector", f_up,
                                waits=[mm_tok, sc_tok, up_free[tsl]], inc=d_v)

                def f_gu(v, hc=hc, tsl=tsl):
                    return v.tensor_mul(out=gu_fm[:, hc, 512 * tsl:512 * tsl + 512],
                                        in0=gs_tmp[:, tsl, :], in1=up_tmp[:, tsl, :])
                gu_tok = b.step("vector", f_gu, waits=[sil_tok, up_tok], inc=d_v)
                bank_free[id(bg)] = gu_tok
                bank_free[id(bu)] = gu_tok
                gs_free[tsl] = gu_tok
                up_free[tsl] = gu_tok
                gu_last = gu_tok
            wcast_free[sl] = [mm_tok]

        wd_free = {0: (d_t, 0), 1: (d_t, 0)}
        wdc_free = {0: (d_t, 0), 1: (d_t, 0)}
        dbank_free = {i: gu_last for i in range(4)}
        rs_free = {0: (d_v, 0), 1: (d_v, 0)}
        fin_free = {0: (d_v, 0), 1: (d_v, 0)}
        i = 0
        sd_free = (d_v, 0)
        for os_ in range(4):
            def f_sd(s, os_=os_):
                return s.dma_start(out=sd_rep[:, :],
                                   in_=sd_hbm[:, 512 * os_:512 * os_ + 512
                                              ].broadcast_to((128, 512)))
            sd_tok = b.step("sync", f_sd, waits=[sd_free], inc=d_sd, amt=16)
            for tb in range(2):
                mm_toks = {t4: None for t4 in range(4)}
                for hc in range(NHC):
                    sl = hc % 2

                    def f_wd(g, hc=hc, os_=os_, sl=sl):
                        return g.dma_start(
                            out=wd_sb[:, sl, :],
                            in_=wd_hbm[128 * hc:128 * hc + 128,
                                       512 * os_:512 * os_ + 512])
                    wdl_tok = b.step("gpsimd", f_wd,
                                     waits=[wd_free[sl], wdc_free[sl]],
                                     inc=(d_wd0 if sl == 0 else d_wd1), amt=16)

                    def f_cd(v, sl=sl):
                        return v.tensor_scalar_add(out=wd_sb[:, sl, :],
                                                   in0=wd_sb[:, sl, :], scalar1=-128.0)
                    cd_tok = b.step("vector", f_cd, waits=[wdl_tok], inc=d_v)

                    def f_mm(t, hc=hc, tb=tb, sl=sl):
                        for t4 in range(4):
                            inst = t.matmul(
                                out=pf[t4][:, :],
                                lhsT=gu_fm[:, hc, 512 * tb + 128 * t4:
                                           512 * tb + 128 * t4 + 128],
                                rhs=wd_sb[:, sl, :],
                                start=(hc == 0), stop=(hc == NHC - 1))
                        return inst
                    mmw = [cd_tok, gu_last]
                    if hc == 0:
                        mmw += [dbank_free[t4] for t4 in range(4)]
                    mm_tok = b.step("tensor", f_mm, waits=mmw, inc=d_t)
                    wd_free[sl] = mm_tok
                    wdc_free[sl] = mm_tok
                    for t4 in range(4):
                        mm_toks[t4] = mm_tok

                for t4 in range(4):
                    sl = i % 2
                    row0 = 512 * tb + 128 * t4

                    def f_ld(s, sem, row0=row0, os_=os_, sl=sl):
                        s.dma_start(out=rs_sb[:, sl, :],
                                    in_=ccout_hbm[row0:row0 + 128,
                                                  512 * os_:512 * os_ + 512]
                                    ).then_inc(sem, 16)
                        s.dma_start(out=xot_sb[:, sl, :],
                                    in_=xown_hbm[row0:row0 + 128,
                                                 512 * os_:512 * os_ + 512]
                                    ).then_inc(sem, 16)
                    ld_tok = b.step("sync", f_ld, waits=[cc_tok, rs_free[sl]],
                                    inc=(d_rs0 if sl == 0 else d_rs1), amt=32,
                                    self_inc=True)

                    def f_f1(v, t4=t4, sl=sl):
                        v.tensor_mul(out=out_sb[:, sl, :], in0=pf[t4][:, :],
                                     in1=sd_rep[:, :])
                        v.drain()
                        v.tensor_add(out=out_sb[:, sl, :], in0=out_sb[:, sl, :],
                                     in1=xot_sb[:, sl, :])
                        v.drain()
                        return v.tensor_add(out=out_sb[:, sl, :], in0=out_sb[:, sl, :],
                                            in1=rs_sb[:, sl, :])
                    f1_tok = b.step("vector", f_f1,
                                    waits=[mm_toks[t4], ld_tok, sd_tok, fin_free[sl]],
                                    inc=d_v)
                    dbank_free[t4] = f1_tok
                    rs_free[sl] = f1_tok

                    def f_out(s, row0=row0, os_=os_, sl=sl):
                        return s.dma_start(out=out_hbm[row0:row0 + 128,
                                                       512 * os_:512 * os_ + 512],
                                           in_=out_sb[:, sl, :])
                    out_tok = b.step("sync", f_out, waits=[f1_tok],
                                     inc=(o_st0 if sl == 0 else o_st1), amt=16)
                    fin_free[sl] = out_tok
                    sd_free = f1_tok
                    i += 1

        b.step("sync", lambda s: s.nop(),
               waits=[(o_st0, o_st0.n), (o_st1, o_st1.n)])

    if stage == "D0d":
        phaseD(d0_only=True)
        _dbg_dump(b, [("d_xn", xn_own[:, :, :], [128, NDC, OWN], BF16)])
        return _finish(b, nc)
    phaseD()
    if stage == "Dd":
        _dbg_dump(b, [("d_xn", xn_own[:, :, :], [128, NDC, OWN], BF16),
                      ("d_gu", gu_fm[:, :, :], [128, NHC, OWN], BF16),
                      ("d_rs", ccout_hbm[:, :], [OWN, HID], F32),
                      ("d_ci", ccin_hbm[:, :], [S, HID], F32)])
    return _finish(b, nc)


def _finish(b, nc):
    blk = nc.Block()
    block = blk.__enter__()
    b._ctxs.append(blk)

    @block.tensor
    def _(eng):
        b.replay("tensor", eng)

    @block.vector
    def _(eng):
        b.replay("vector", eng)

    @block.scalar
    def _(eng):
        b.replay("scalar", eng)

    @block.gpsimd
    def _(eng):
        b.replay("gpsimd", eng)

    @block.sync
    def _(eng):
        b.replay("sync", eng)

    blk.__exit__(None, None, None)
    b._ctxs.pop()
    return b


def _dbg_dump(b, specs):
    nc = b.nc
    dsem = b.sem("dbg_st")
    for name, ap, shape, dtype in specs:
        hbm = nc.dram_tensor(name, shape, dtype, kind="ExternalOutput").ap()
        b.dbg_outs.append(name)

        def f(s, sem, hbm=hbm, ap=ap):
            s.dma_start(out=hbm, in_=ap).then_inc(sem, 16)
        b.step("sync", f, waits=[], inc=dsem, amt=16, self_inc=True)
    b.step("sync", lambda s: s.nop(), waits=[(dsem, dsem.n)])


_BUILD_CACHE = {}


def _get_build(dbg=(), stage="full"):
    key = (tuple(sorted(dbg)), stage)
    if key not in _BUILD_CACHE:
        _BUILD_CACHE[key] = build(dbg, stage)
    return _BUILD_CACHE[key]


def make_in_maps(inputs):
    x = np.ascontiguousarray(np.asarray(inputs["L0_attn_x"], dtype=np.float32))
    w_qkv = np.asarray(inputs["L0_attn_w_qkv"], dtype=np.float32)
    w_o = np.asarray(inputs["L0_attn_w_o"], dtype=np.float32)
    qn_g = np.asarray(inputs["L0_attn_qn_g"], dtype=np.float32).reshape(1, D)
    qn_b = np.asarray(inputs["L0_attn_qn_b"], dtype=np.float32).reshape(1, D)
    kn_g = np.asarray(inputs["L0_attn_kn_g"], dtype=np.float32).reshape(1, D)
    kn_b = np.asarray(inputs["L0_attn_kn_b"], dtype=np.float32).reshape(1, D)
    wg_t = np.ascontiguousarray(np.asarray(inputs["L0_ffn_wq_gate"], dtype=np.int32).T)
    wu_t = np.ascontiguousarray(np.asarray(inputs["L0_ffn_wq_up"], dtype=np.int32).T)
    wd_t = np.ascontiguousarray(np.asarray(inputs["L0_ffn_wq_down"], dtype=np.int32).T)
    s_gate = np.asarray(inputs["L0_ffn_s_gate"], dtype=np.float32).reshape(1, FH)
    s_up = np.asarray(inputs["L0_ffn_s_up"], dtype=np.float32).reshape(1, FH)
    s_down = np.asarray(inputs["L0_ffn_s_down"], dtype=np.float32).reshape(1, HID)

    import ml_dtypes
    ident = np.eye(128, dtype=np.float32).astype(ml_dtypes.bfloat16)
    iot64 = np.arange(64, dtype=np.float32).reshape(1, 64)
    tcol_h = (np.arange(128, dtype=np.float32)[:, None]
              + 128.0 * np.arange(NTT, dtype=np.float32)[None, :])
    tcol_h = np.ascontiguousarray(tcol_h)
    tri = np.where(np.arange(128)[None, :] > np.arange(128)[:, None],
                   np.float32(NEG_BIG), np.float32(0.0)).astype(np.float32)

    wqkv_T = np.ascontiguousarray(w_qkv.T)
    wo_T = np.ascontiguousarray(w_o.T)

    in_maps = []
    for c in range(8):
        g, r = c // 4, c % 4
        cols = np.concatenate([
            np.arange(512 * r, 512 * r + 512),
            np.arange(2048 + 128 * r, 2048 + 128 * r + 128),
            np.arange(2560 + 128 * r, 2560 + 128 * r + 128),
        ])
        wqkv_slice = np.ascontiguousarray(wqkv_T[:, cols])
        wo_slice = np.ascontiguousarray(wo_T[512 * r:512 * r + 512, :])
        in_maps.append({
            "x": x[g],
            "x_own": np.ascontiguousarray(x[g, 1024 * r:1024 * (r + 1)]),
            "wqkv_t": wqkv_slice,
            "wo_t": wo_slice,
            "qn_g": qn_g, "qn_b": qn_b, "kn_g": kn_g, "kn_b": kn_b,
            "wg_t": wg_t, "wu_t": wu_t, "wd_t": wd_t,
            "s_gate": s_gate, "s_up": s_up, "s_down": s_down,
            "tri": tri, "ident": ident, "iot64": iot64, "tcol_h": tcol_h,
        })
    return in_maps


def kernel(**inputs):
    b = _get_build()
    in_maps = make_in_maps(inputs)
    res = run_bass_kernel_spmd(b.nc, in_maps, core_ids=list(range(8)))
    out = np.empty((2, S, HID), dtype=np.float32)
    for c in range(8):
        g, r = c // 4, c % 4
        out[g, 1024 * r:1024 * (r + 1)] = res.results[c]["out"]
    return out


# version 04
# speedup vs baseline: 1.0097x; 1.0097x over previous
import math

import numpy as np
import concourse.bass as bass
import concourse.mybir as mybir
from concourse.bass_utils import run_bass_kernel_spmd
from concourse.replica_groups import maybe_share_collective_output_space

F32 = mybir.dt.float32
BF16 = mybir.dt.bfloat16
I32 = mybir.dt.int32
AF = mybir.ActivationFunctionType
ALU = mybir.AluOpType
AX = mybir.AxisListType.X

S = 4096
HID = 2048
NH = 4
D = 128
FH = 5632
NTT = S // 128
NHC = FH // 128
NDC = HID // 128
OWN = 1024

ISD = 1.0 / math.sqrt(D)
TWO_PI = 2.0 * math.pi
C_EPS = 1e-5
C_PI2 = math.pi / 2.0
C_NEG1 = -1.0
C_IVF = -2.0 * math.log(1.0e6) / D
NEG_BIG = -1.0e30

ENGS = ("tensor", "vector", "scalar", "gpsimd", "sync")


class Sem:
    __slots__ = ("h", "n", "name")

    def __init__(self, h, name):
        self.h = h
        self.n = 0
        self.name = name


class B:

    def __init__(self, dbg=()):
        self.dbg = set(dbg)
        self.nc = bass.Bass("TRN2", target_bir_lowering=False, detect_race_conditions=False)
        self.steps = {e: [] for e in ENGS}
        self._ctxs = []
        self.sems = {}
        self.dbg_outs = []

    def sem(self, name):
        if name not in self.sems:
            g = self.nc.semaphore(name)
            h = g.__enter__()
            self._ctxs.append(g)
            self.sems[name] = Sem(h, name)
        return self.sems[name]

    def sbuf(self, name, shape, dtype):
        g = self.nc.sbuf_tensor(name, shape, dtype)
        t = g.__enter__()
        self._ctxs.append(g)
        return t

    def psum(self, name, shape, dtype):
        g = self.nc.psum_tensor(name, shape, dtype)
        t = g.__enter__()
        self._ctxs.append(g)
        return t

    def step(self, eng, emit, waits=(), inc=None, amt=1, self_inc=False):
        ws = tuple((s, v) for (s, v) in waits if s is not None and v > 0)
        tok = None
        if inc is not None:
            inc.n += amt
            tok = (inc, inc.n)
        self.steps[eng].append((ws, emit, (inc.h if inc else None), amt, self_inc))
        return tok

    def replay(self, eng_name, eng):
        fence = False
        for ws, emit, inch, amt, self_inc in self.steps[eng_name]:
            for (s, v) in ws:
                eng.wait_ge(s.h, v)
            if self_inc:
                emit(eng, inch)
            else:
                inst = emit(eng)
                if inch is not None:
                    assert inst is not None, f"step on {eng_name} with inc returned None"
                    if fence:
                        inst = eng.drain()
                    inst.then_inc(inch, amt)


def build(dbg=(), stage="full", cut=99):
    b = B(dbg)
    nc = b.nc

    x_hbm = nc.dram_tensor("x", [S, HID], F32, kind="ExternalInput").ap()
    xown_hbm = nc.dram_tensor("x_own", [OWN, HID], F32, kind="ExternalInput").ap()
    wqkv_hbm = nc.dram_tensor("wqkv_t", [HID, 768], F32, kind="ExternalInput").ap()
    wo_hbm = nc.dram_tensor("wo_t", [512, HID], F32, kind="ExternalInput").ap()
    qng_hbm = nc.dram_tensor("qn_g", [1, D], F32, kind="ExternalInput").ap()
    qnb_hbm = nc.dram_tensor("qn_b", [1, D], F32, kind="ExternalInput").ap()
    kng_hbm = nc.dram_tensor("kn_g", [1, D], F32, kind="ExternalInput").ap()
    knb_hbm = nc.dram_tensor("kn_b", [1, D], F32, kind="ExternalInput").ap()
    wg_hbm = nc.dram_tensor("wg_t", [HID, FH], I32, kind="ExternalInput").ap()
    wu_hbm = nc.dram_tensor("wu_t", [HID, FH], I32, kind="ExternalInput").ap()
    wd_hbm = nc.dram_tensor("wd_t", [FH, HID], I32, kind="ExternalInput").ap()
    sg_hbm = nc.dram_tensor("s_gate", [1, FH], F32, kind="ExternalInput").ap()
    su_hbm = nc.dram_tensor("s_up", [1, FH], F32, kind="ExternalInput").ap()
    sd_hbm = nc.dram_tensor("s_down", [1, HID], F32, kind="ExternalInput").ap()
    tri_hbm = nc.dram_tensor("tri", [128, 128], F32, kind="ExternalInput").ap()
    iot_hbm = nc.dram_tensor("iot64", [1, 64], F32, kind="ExternalInput").ap()
    tcol_hbm = nc.dram_tensor("tcol_h", [128, NTT], F32, kind="ExternalInput").ap()
    id_hbm = nc.dram_tensor("ident", [128, 128], BF16, kind="ExternalInput").ap()

    out_hbm = nc.dram_tensor("out", [OWN, HID], F32, kind="ExternalOutput").ap()

    rg = [[0, 1, 2, 3], [4, 5, 6, 7]]
    cc_space = maybe_share_collective_output_space("ReduceScatter", rg)
    ccin_hbm = nc.dram_tensor("cc_in", [S, HID], F32, kind="Internal").ap()
    ccout_hbm = nc.dram_tensor("cc_out", [OWN, HID], F32, kind="Internal",
                               addr_space=cc_space).ap()

    ident = b.sbuf("ident_sb", [128, 128], BF16)
    tri = b.sbuf("tri_sb", [128, 128], F32)
    qg_rep = b.sbuf("qg_rep", [128, D], F32)
    qb_rep = b.sbuf("qb_rep", [128, D], F32)
    kg_rep = b.sbuf("kg_rep", [128, D], F32)
    kb_rep = b.sbuf("kb_rep", [128, D], F32)
    cst = b.sbuf("cst", [128, 8], F32)
    invfreq = b.sbuf("invfreq", [128, 64], F32)
    invfreq_i = b.sbuf("invfreq_i", [128, 64], F32)

    ARENA_KB = 200
    arena = b.sbuf("arena", [128, ARENA_KB * 1024], mybir.dt.uint8)
    arena_base = nc.lookup_mloc(arena).addr

    def at(off, name, shape, dtype):
        per_part = int(np.prod(shape[1:])) * mybir.dt.size(dtype)
        off = (off + 31) // 32 * 32
        assert off + per_part <= ARENA_KB * 1024, (name, off, per_part)
        t = nc.alloc_sbuf_tensor_at(name, shape, dtype, offset=arena_base + off)
        return t, off + per_part

    o = 0
    q_fm, o = at(o, "q_fm", [128, NH, NTT, 128], BF16)
    k_fm, o = at(o, "k_fm", [128, NTT * 128], BF16)
    v_tm, o = at(o, "v_tm", [128, NTT, 128], BF16)
    attn_fm, o = at(o, "attn_fm", [128, NH, NTT, 128], BF16)
    wqkv_sb, o = at(o, "wqkv_sb", [128, NDC, 768], BF16)
    wo_sb, o = at(o, "wo_sb", [128, 4, HID], BF16)
    AB_END = o

    o = AB_END
    x_sb, o = at(o, "x_sb", [128, 2, HID], F32)
    xn_tm, o = at(o, "xn_tm", [128, 2, HID], BF16)
    xn_fm, o = at(o, "xn_fm", [128, 2, NDC, 128], BF16)
    cos_tab, o = at(o, "cos_tab", [128, NTT, D], BF16)
    sin_tab, o = at(o, "sin_tab", [128, NTT, D], BF16)
    tcol, o = at(o, "tcol", [128, NTT], F32)
    sqd, o = at(o, "sqd", [128, 2, 640], F32)
    st_a, o = at(o, "st_a", [128, 2, 40], F32)
    xc_s, o = at(o, "xc_s", [128, 2, 640], F32)
    y_s, o = at(o, "y_s", [128, 2, 640], F32)
    r1_s, o = at(o, "r1_s", [128, 2, 640], F32)
    r2_s, o = at(o, "r2_s", [128, 2, 640], F32)
    qk_tm, o = at(o, "qk_tm", [128, 2, 640], BF16)
    sqx, o = at(o, "sqx", [128, HID], BF16)
    assert o <= ARENA_KB * 1024

    o = AB_END
    p_sb, o = at(o, "p_sb", [128, 2, 512], BF16)
    pt_sb, o = at(o, "pt_sb", [128, 2, 512], BF16)
    mcol, o = at(o, "mcol", [128, 4, 8], F32)
    lcol, o = at(o, "lcol", [128, 4, 8], F32)
    mrow, o = at(o, "mrow", [128, 4, 4], F32)
    atm, o = at(o, "atm", [128, 4, 128], BF16)
    po_sb, o = at(o, "po_sb", [128, 2, 512], F32)
    assert o <= ARENA_KB * 1024

    o = 0
    xn_own, o = at(o, "xn_own", [128, NDC, OWN], BF16)
    gu_fm, o = at(o, "gu_fm", [128, NHC, OWN], BF16)
    wgI, o = at(o, "wgI", [128, NDC, 128], I32)
    wuI, o = at(o, "wuI", [128, NDC, 128], I32)
    wg_sb, o = at(o, "wg_sb", [128, 2, NDC, 128], BF16)
    wu_sb, o = at(o, "wu_sb", [128, 2, NDC, 128], BF16)
    wd_sb, o = at(o, "wd_sb", [128, 2, 512], BF16)
    sd_rep, o = at(o, "sd_rep", [128, 512], F32)
    sg_fm, o = at(o, "sg_fm", [128, NHC], F32)
    su_fm, o = at(o, "su_fm", [128, NHC], F32)
    xo_sb, o = at(o, "xo_sb", [128, HID], F32)
    xno_tm, o = at(o, "xno_tm", [128, 2, HID], BF16)
    gs_tmp, o = at(o, "gs_tmp", [128, 2, 512], BF16)
    up_tmp, o = at(o, "up_tmp", [128, 2, 512], BF16)
    rs_sb, o = at(o, "rs_sb", [128, 2, 512], F32)
    xot_sb, o = at(o, "xot_sb", [128, 2, 512], F32)
    out_sb, o = at(o, "out_sb", [128, 2, 512], F32)
    st_d, o = at(o, "st_d", [128, 2, 8], F32)
    sqd_d, o = at(o, "sqd_d", [128, HID], BF16)
    assert o <= ARENA_KB * 1024

    pf = [b.psum(f"pf{i}", [128, 512], F32) for i in range(6)]
    pbank = [b.psum(f"pb{i}", [128, 1024], BF16) for i in range(2)]
    pbf = [pbank[i][:, :].bitcast(F32) for i in range(2)]
    pf4b = pf[4][:, :].bitcast(BF16)
    pf5b = pf[5][:, :].bitcast(BF16)

    for i, v in enumerate((C_EPS, ISD, C_PI2, C_NEG1, C_IVF)):
        nc.const_aps.register(F32, v, cst[:, i:i + 1])

    s_ld = b.sem("s_ld")
    s_iot = b.sem("s_iot")
    g_ld = b.sem("g_ld")
    a_x0 = b.sem("a_x0")
    a_x1 = b.sem("a_x1")
    bar = b.sem("bar")
    a_fin = b.sem("a_fin")
    a_v = b.sem("a_v")
    a_s = b.sem("a_s")
    a_g = b.sem("a_g")
    a_t = b.sem("a_t")
    b_fin = b.sem("b_fin")
    b_t = b.sem("b_t")
    b_v = b.sem("b_v")
    b_s = b.sem("b_s")
    c_t = b.sem("c_t")
    c_v = b.sem("c_v")
    c_s = b.sem("c_s")
    c_st0 = b.sem("c_st0")
    c_st1 = b.sem("c_st1")
    cc_done = b.sem("cc_done")
    d_sc = b.sem("d_sc")
    d_xo = b.sem("d_xo")
    d_wgu = b.sem("d_wgu")
    d_wgu2 = b.sem("d_wgu2")
    d_wd0 = b.sem("d_wd0")
    d_wd1 = b.sem("d_wd1")
    d_rs0 = b.sem("d_rs0")
    d_rs1 = b.sem("d_rs1")
    d_sd = b.sem("d_sd")
    d_v = b.sem("d_v")
    d_s = b.sem("d_s")
    d_t = b.sem("d_t")
    d_fin = b.sem("d_fin")
    o_st0 = b.sem("o_st0")
    o_st1 = b.sem("o_st1")

    def barrier():
        b.step("tensor", lambda t: t.ldweights(ident[:, :]), inc=bar, amt=1)
        b.step("vector", lambda v: v.memset(cst[:, 5:6], 0.0), inc=bar, amt=1)
        b.step("scalar", lambda s: s.activation(out=cst[:, 6:7], in_=cst[:, 6:7],
                                                func=AF.Copy), inc=bar, amt=1)
        b.step("gpsimd", lambda g: g.memset(cst[:, 7:8], 0.0), inc=bar, amt=1)
        b.step("sync", lambda s: s.nop(), inc=bar, amt=1)
        want = bar.n
        for e in ENGS:
            b.step(e, lambda x: x.nop(), waits=[(bar, want)])

    def ph0():
        def f_sync(s, sem):
            s.dma_start(out=tri[:, :], in_=tri_hbm).then_inc(sem, 16)
            s.dma_start(out=ident[:, :], in_=id_hbm).then_inc(sem, 16)
            s.dma_start(out=qg_rep[:, :], in_=qng_hbm.broadcast_to((128, D))).then_inc(sem, 16)
            s.dma_start(out=qb_rep[:, :], in_=qnb_hbm.broadcast_to((128, D))).then_inc(sem, 16)
            s.dma_start(out=kg_rep[:, :], in_=kng_hbm.broadcast_to((128, D))).then_inc(sem, 16)
            s.dma_start(out=kb_rep[:, :], in_=knb_hbm.broadcast_to((128, D))).then_inc(sem, 16)
        b.step("sync", f_sync, inc=s_ld, amt=6 * 16, self_inc=True)
        tok_small = (s_ld, s_ld.n)

        def f_gp(g, sem):
            src = wqkv_hbm.rearrange("(c p) f -> p c f", p=128)
            for i in range(4):
                g.dma_start(out=wqkv_sb[:, 4 * i:4 * i + 4, :],
                            in_=src[:, 4 * i:4 * i + 4, :]).then_inc(sem, 16)
            g.dma_start(out=wo_sb[:, :, :],
                        in_=wo_hbm.rearrange("(c p) f -> p c f", p=128)).then_inc(sem, 16)
        b.step("gpsimd", f_gp, inc=g_ld, amt=5 * 16, self_inc=True)
        tok_w = (g_ld, g_ld.n)

        def f_vec(v):
            inst = None
            for i, val in enumerate((C_EPS, ISD, C_PI2, C_NEG1, C_IVF, 0.0, 0.0, 0.0)):
                inst = v.memset(cst[:, i:i + 1], val)
            return inst
        b.step("vector", f_vec, inc=a_v)
        tok_cst = (a_v, a_v.n)

        def f_gp2(s, sem):
            s.dma_start(out=invfreq_i[:, :], in_=iot_hbm.broadcast_to((128, 64))
                        ).then_inc(sem, 16)
            s.dma_start(out=tcol[:, :], in_=tcol_hbm).then_inc(sem, 16)
        b.step("sync", f_gp2, inc=s_iot, amt=32, self_inc=True)
        tok_if = (s_iot, s_iot.n)

        def f_scl(s):
            return s.activation(out=invfreq[:, :], in_=invfreq_i[:, :], func=AF.Exp,
                                scale=cst[:, 4:5])
        b.step("scalar", f_scl, waits=[tok_if, tok_cst], inc=a_s)
        return tok_small, tok_w, (a_s, a_s.n)

    tok_small, tok_w, tok_if = ph0()

    def phaseA(ntiles=NTT, cut=99):
        t_tok = tok_if
        for tt in range(NTT):
            bf = tt % 2

            def f_vr(v, tt=tt, bf=bf):
                r2 = r2_s[:, bf, :]

                def reduce_angle(ang, o):
                    u = r2[:, o:o + 64]
                    ki = r2[:, o + 64:o + 128].bitcast(I32)
                    kf = r2[:, o + 128:o + 192]
                    v.tensor_scalar_mul(out=u, in0=ang, scalar1=1.0 / TWO_PI)
                    v.drain()
                    v.tensor_copy(out=ki, in_=u)
                    v.drain()
                    v.tensor_copy(out=kf, in_=ki)
                    phi = r2[:, o + 192:o + 256]
                    v.drain()
                    v.scalar_tensor_tensor(out=phi, in0=kf, scalar=-TWO_PI, in1=ang,
                                           op0=ALU.mult, op1=ALU.add)
                    m = r2[:, o:o + 64]
                    v.drain()
                    v.tensor_single_scalar(out=m, in_=phi, scalar=math.pi, op=ALU.is_gt)
                    v.drain()
                    v.scalar_tensor_tensor(out=phi, in0=m, scalar=-TWO_PI, in1=phi,
                                           op0=ALU.mult, op1=ALU.add)
                    v.drain()
                    v.tensor_single_scalar(out=m, in_=phi, scalar=-math.pi, op=ALU.is_lt)
                    v.drain()
                    return v.scalar_tensor_tensor(out=phi, in0=m, scalar=TWO_PI, in1=phi,
                                                  op0=ALU.mult, op1=ALU.add)
                th = r1_s[:, bf, 0:64]
                v.tensor_scalar(out=th, in0=invfreq[:, :], scalar1=tcol[:, tt:tt + 1],
                                scalar2=None, op0=ALU.mult)
                v.drain()
                reduce_angle(th, 0)
                ac = r1_s[:, bf, 64:128]
                v.drain()
                v.tensor_scalar_add(out=ac, in0=th, scalar1=C_PI2)
                v.drain()
                return reduce_angle(ac, 256)
            prev = (a_s, a_s.n) if tt >= 2 else None
            ang_tok = b.step("vector", f_vr,
                             waits=[t_tok, tok_if] + ([prev] if prev else []), inc=a_v)

            def f_sr(sc, tt=tt, bf=bf):
                phi = r2_s[:, bf, 192:256]
                psi = r2_s[:, bf, 448:512]
                sc.activation(out=sin_tab[:, tt, 0:64], in_=phi, func=AF.Sin,
                              scale=cst[:, 3:4])
                sc.activation(out=sin_tab[:, tt, 64:128], in_=phi, func=AF.Sin)
                cdup = cos_tab[:, tt, :].rearrange("p (r i) -> p r i", r=2)
                psb = psi.unsqueeze(1).broadcast_to((128, 2, 64))
                return sc.activation(out=cdup, in_=psb, func=AF.Sin)
            b.step("scalar", f_sr, waits=[ang_tok], inc=a_s)
        rope_done = (a_s, a_s.n)

        pb0_free = {0: None, 1: None}
        pb1_free = None
        for tt in range(ntiles):
            bf = tt % 2
            blanket = (a_fin, 3 * (tt - 1)) if tt >= 2 else None
            bl = [blanket] if blanket else []

            def f_x(s, tt=tt, bf=bf):
                src = x_hbm.rearrange("(n p) d -> p n d", p=128)
                return s.dma_start(out=x_sb[:, bf, :], in_=src[:, tt, :])
            x_tok = b.step("sync", f_x, waits=bl, inc=(a_x0 if bf == 0 else a_x1),
                           amt=16)

            last = {"tensor": None, "vector": None, "scalar": None, "gpsimd": None}
            if cut < 2:
                b.step("sync", lambda s, x_tok=x_tok: s.nop(), waits=[x_tok])
                for e in ("tensor", "vector", "scalar", "gpsimd"):
                    b.step(e, lambda x: x.nop(), waits=[x_tok], inc=a_fin)
                continue
            def f_v1(v, bf=bf):
                return v.reduce_sum(out=st_a[:, bf, 0:1], in_=x_sb[:, bf, :], axis=AX)
            sum_tok = b.step("vector", f_v1, waits=[x_tok], inc=a_v)

            def f_s1(s, bf=bf):
                return s.activation(out=sqx[:, :], in_=x_sb[:, bf, :], func=AF.Square,
                                    accum_out=st_a[:, bf, 1:2])
            ssq_tok = b.step("scalar", f_s1, waits=[x_tok], inc=a_s)

            def f_v2(v, bf=bf):
                stx = st_a[:, bf, :]
                v.tensor_scalar_mul(out=stx[:, 2:3], in0=stx[:, 0:1], scalar1=1.0 / HID)
                v.drain()
                v.tensor_tensor(out=stx[:, 3:4], in0=stx[:, 2:3], in1=stx[:, 2:3],
                                op=ALU.mult)
                v.drain()
                return v.scalar_tensor_tensor(out=stx[:, 4:5], in0=stx[:, 1:2],
                                              scalar=1.0 / HID, in1=stx[:, 3:4],
                                              op0=ALU.mult, op1=ALU.subtract)
            var_tok = b.step("vector", f_v2, waits=[ssq_tok, sum_tok], inc=a_v)
            if cut < 3:
                b.step("tensor", lambda t: t.nop(), waits=[var_tok], inc=a_fin)
                b.step("vector", lambda v: v.nop(), waits=[var_tok], inc=a_fin)
                b.step("scalar", lambda s: s.nop(), waits=[ssq_tok], inc=a_fin)
                b.step("gpsimd", lambda g: g.nop(), waits=[var_tok], inc=a_fin)
                continue

            def f_s2(s, bf=bf):
                return s.activation(out=st_a[:, bf, 5:6], in_=st_a[:, bf, 4:5],
                                    func=AF.Sqrt, bias=cst[:, 0:1])
            std_tok = b.step("scalar", f_s2, waits=[var_tok], inc=a_s)

            def f_v3(v, bf=bf):
                v.reciprocal(out=st_a[:, bf, 6:7], in_=st_a[:, bf, 5:6])
                v.drain()
                return v.tensor_scalar(out=st_a[:, bf, 7:8], in0=st_a[:, bf, 2:3],
                                       scalar1=st_a[:, bf, 6:7], scalar2=-1.0,
                                       op0=ALU.mult, op1=ALU.mult)
            inv_tok = b.step("vector", f_v3, waits=[std_tok], inc=a_v)

            def f_g1(sc, bf=bf):
                return sc.activation(out=xn_tm[:, bf, :], in_=x_sb[:, bf, :],
                                     func=AF.Identity, scale=st_a[:, bf, 6:7],
                                     bias=st_a[:, bf, 7:8])
            xn_tok = b.step("scalar", f_g1, waits=[inv_tok], inc=a_s)
            if cut < 5:
                b.step("tensor", lambda t: t.nop(), waits=[xn_tok], inc=a_fin)
                b.step("vector", lambda v: v.nop(), waits=[xn_tok], inc=a_fin)
                b.step("scalar", lambda s: s.nop(), waits=[std_tok], inc=a_fin)
                b.step("gpsimd", lambda g: g.nop(), waits=[xn_tok], inc=a_fin)
                continue

            grp_toks = []
            for grp in range(4):
                half = grp % 2

                def f_tr(t, bf=bf, grp=grp, half=half):
                    for j in range(4):
                        dc = grp * 4 + j
                        inst = t.transpose(out=pbank[half][:, 128 * j:128 * j + 128],
                                           in_=xn_tm[:, bf, 128 * dc:128 * dc + 128],
                                           identity=ident[:, :])
                    return inst
                w = [xn_tok, tok_small]
                if pb0_free[half] is not None:
                    w.append(pb0_free[half])
                tr_tok = b.step("tensor", f_tr, waits=w, inc=a_t)

                def f_ev(v, bf=bf, grp=grp, half=half):
                    return v.tensor_copy(
                        out=xn_fm[:, bf, grp * 4:(grp + 1) * 4, :],
                        in_=pbank[half][:, 0:512].rearrange("p (c i) -> p c i", c=4))
                ev_tok = b.step("vector", f_ev, waits=[tr_tok], inc=a_v)
                pb0_free[half] = ev_tok
                grp_toks.append(ev_tok)

            bq = pf[0 + bf]
            bkv = pf[2 + bf]

            def f_mm(t, bf=bf, bq=bq, bkv=bkv):
                for dc in range(NDC):
                    lhs = xn_fm[:, bf, dc, :]
                    t.matmul(out=bq[:, :], lhsT=lhs, rhs=wqkv_sb[:, dc, 0:512],
                             start=(dc == 0), stop=(dc == NDC - 1))
                    inst = t.matmul(out=bkv[:, 0:256], lhsT=lhs,
                                    rhs=wqkv_sb[:, dc, 512:768],
                                    start=(dc == 0), stop=(dc == NDC - 1))
                return inst
            if cut < 6:
                b.step("tensor", lambda t, w=grp_toks[-1]: t.nop(), waits=[grp_toks[-1]], inc=a_fin)
                b.step("vector", lambda v: v.nop(), waits=[grp_toks[-1]], inc=a_fin)
                b.step("scalar", lambda s: s.nop(), waits=[std_tok], inc=a_fin)
                b.step("gpsimd", lambda g: g.nop(), waits=[xn_tok], inc=a_fin)
                continue
            mm_tok = b.step("tensor", f_mm, waits=[grp_toks[-1], tok_w], inc=a_t)

            if cut < 7:
                b.step("tensor", lambda t: t.nop(), waits=[mm_tok], inc=a_fin)
                b.step("vector", lambda v: v.nop(), waits=[mm_tok], inc=a_fin)
                b.step("scalar", lambda s: s.nop(), waits=[std_tok], inc=a_fin)
                b.step("gpsimd", lambda g: g.nop(), waits=[xn_tok], inc=a_fin)
                continue
            def f_v5(v, bf=bf, tt=tt, bq=bq, bkv=bkv):
                v.tensor_copy(out=v_tm[:, tt, :], in_=bkv[:, 128:256])
                stx = st_a[:, bf, :]
                v.reduce_sum(out=stx[:, 8:12],
                             in_=bq[:, :].rearrange("p (h d) -> p h d", h=4), axis=AX)
                return v.reduce_sum(out=stx[:, 12:13], in_=bkv[:, 0:128], axis=AX)
            qsum_tok = b.step("vector", f_v5, waits=[mm_tok], inc=a_v)
            if cut < 62:
                b.step("tensor", lambda t: t.nop(), waits=[qsum_tok], inc=a_fin)
                b.step("vector", lambda v: v.nop(), waits=[qsum_tok], inc=a_fin)
                b.step("scalar", lambda s: s.nop(), waits=[std_tok], inc=a_fin)
                b.step("gpsimd", lambda g: g.nop(), waits=[xn_tok], inc=a_fin)
                continue

            def f_s3(s, bf=bf, bq=bq, bkv=bkv):
                s.activation(out=sqd[:, bf, 0:512], in_=bq[:, :], func=AF.Square)
                return s.activation(out=sqd[:, bf, 512:640], in_=bkv[:, 0:128],
                                    func=AF.Square)
            sq_tok = b.step("scalar", f_s3, waits=[mm_tok, qsum_tok], inc=a_s)
            if cut < 63:
                b.step("tensor", lambda t: t.nop(), waits=[sq_tok], inc=a_fin)
                b.step("vector", lambda v: v.nop(), waits=[qsum_tok], inc=a_fin)
                b.step("scalar", lambda s: s.nop(), waits=[sq_tok], inc=a_fin)
                b.step("gpsimd", lambda g: g.nop(), waits=[xn_tok], inc=a_fin)
                continue

            def f_v6(v, bf=bf):
                stx = st_a[:, bf, :]
                v.reduce_sum(out=stx[:, 13:18],
                             in_=sqd[:, bf, :].rearrange("p (h d) -> p h d", h=5),
                             axis=AX)
                v.tensor_scalar_mul(out=stx[:, 18:23], in0=stx[:, 8:13], scalar1=1.0 / D)
                v.drain()
                v.tensor_tensor(out=stx[:, 23:28], in0=stx[:, 18:23], in1=stx[:, 18:23],
                                op=ALU.mult)
                v.drain()
                return v.scalar_tensor_tensor(out=stx[:, 28:33], in0=stx[:, 13:18],
                                              scalar=1.0 / D, in1=stx[:, 23:28],
                                              op0=ALU.mult, op1=ALU.subtract)
            qvar_tok = b.step("vector", f_v6, waits=[sq_tok, qsum_tok], inc=a_v)
            if cut < 64:
                b.step("tensor", lambda t: t.nop(), waits=[qvar_tok], inc=a_fin)
                b.step("vector", lambda v: v.nop(), waits=[qvar_tok], inc=a_fin)
                b.step("scalar", lambda s: s.nop(), waits=[sq_tok], inc=a_fin)
                b.step("gpsimd", lambda g: g.nop(), waits=[xn_tok], inc=a_fin)
                continue

            def f_s4(s, bf=bf):
                return s.activation(out=st_a[:, bf, 33:38], in_=st_a[:, bf, 28:33],
                                    func=AF.Sqrt, bias=cst[:, 0:1])
            qstd_tok = b.step("scalar", f_s4, waits=[qvar_tok], inc=a_s)
            if cut < 65:
                b.step("tensor", lambda t: t.nop(), waits=[qvar_tok], inc=a_fin)
                b.step("vector", lambda v: v.nop(), waits=[qvar_tok], inc=a_fin)
                b.step("scalar", lambda s: s.nop(), waits=[qstd_tok], inc=a_fin)
                b.step("gpsimd", lambda g: g.nop(), waits=[xn_tok], inc=a_fin)
                continue

            def f_v7(v, bf=bf, bq=bq, bkv=bkv):
                stx = st_a[:, bf, :]
                v.reciprocal(out=stx[:, 23:28], in_=stx[:, 33:38])
                v.drain()
                inst = None
                for seg in range(5):
                    src = bq[:, 128 * seg:128 * seg + 128] if seg < 4 else bkv[:, 0:128]
                    inst = v.tensor_scalar(out=xc_s[:, bf, 128 * seg:128 * seg + 128],
                                           in0=src,
                                           scalar1=stx[:, 18 + seg:19 + seg],
                                           scalar2=stx[:, 23 + seg:24 + seg],
                                           op0=ALU.subtract, op1=ALU.mult)
                return inst
            xc_tok = b.step("vector", f_v7, waits=[qstd_tok], inc=a_v)
            if cut < 8:
                b.step("tensor", lambda t: t.nop(), waits=[xc_tok], inc=a_fin)
                b.step("vector", lambda v: v.nop(), waits=[xc_tok], inc=a_fin)
                b.step("scalar", lambda s: s.nop(), waits=[qstd_tok], inc=a_fin)
                b.step("gpsimd", lambda g: g.nop(), waits=[xc_tok], inc=a_fin)
                continue

            def f_g2(v, bf=bf):
                gq = qg_rep[:, :].unsqueeze(1).broadcast_to((128, 4, D))
                bqr = qb_rep[:, :].unsqueeze(1).broadcast_to((128, 4, D))
                xq = xc_s[:, bf, 0:512].rearrange("p (h d) -> p h d", h=4)
                yq = y_s[:, bf, 0:512].rearrange("p (h d) -> p h d", h=4)
                v.tensor_tensor(out=yq, in0=xq, in1=gq, op=ALU.mult)
                v.tensor_tensor(out=y_s[:, bf, 512:640], in0=xc_s[:, bf, 512:640],
                                in1=kg_rep[:, :], op=ALU.mult)
                v.drain()
                v.tensor_tensor(out=yq, in0=yq, in1=bqr, op=ALU.add)
                return v.tensor_tensor(out=y_s[:, bf, 512:640], in0=y_s[:, bf, 512:640],
                                       in1=kb_rep[:, :], op=ALU.add)
            y_tok = b.step("vector", f_g2, waits=[xc_tok, tok_small], inc=a_v)

            def f_v8(v, bf=bf, tt=tt):
                cos5 = cos_tab[:, tt, :].unsqueeze(1).broadcast_to((128, 5, D))
                sin5 = sin_tab[:, tt, :].unsqueeze(1).broadcast_to((128, 5, D))
                yv = y_s[:, bf, :].rearrange("p (s d) -> p s d", s=5)
                yrot = y_s[:, bf, :].rearrange("p (s j i) -> p s j i", s=5, j=2)[:, :, ::-1, :]
                t1 = r1_s[:, bf, :].rearrange("p (s d) -> p s d", s=5)
                t2 = r2_s[:, bf, :].rearrange("p (s j i) -> p s j i", s=5, j=2)
                sin4 = sin5.rearrange("p s (j i) -> p s j i", j=2)
                v.tensor_tensor(out=t1, in0=yv, in1=cos5, op=ALU.mult)
                v.tensor_tensor(out=t2, in0=yrot, in1=sin4, op=ALU.mult)
                v.drain()
                return v.tensor_tensor(out=qk_tm[:, bf, :].rearrange("p (s d) -> p s d", s=5),
                                       in0=t1, in1=t2, op=ALU.add)
            qk_tok = b.step("vector", f_v8, waits=[y_tok, rope_done], inc=a_v)
            if cut < 10:
                b.step("tensor", lambda t: t.nop(), waits=[qk_tok], inc=a_fin)
                b.step("vector", lambda v: v.nop(), waits=[qk_tok], inc=a_fin)
                b.step("scalar", lambda s: s.nop(), waits=[qstd_tok], inc=a_fin)
                continue

            def f_t3(t, bf=bf):
                for seg in range(5):
                    inst = t.transpose(out=pf4b[:, 128 * seg:128 * seg + 128],
                                       in_=qk_tm[:, bf, 128 * seg:128 * seg + 128],
                                       identity=ident[:, :])
                return inst
            w = [qk_tok]
            if pb1_free is not None:
                w.append(pb1_free)
            qkt_tok = b.step("tensor", f_t3, waits=w, inc=a_t)

            def f_v9(v, tt=tt):
                v.tensor_copy(out=q_fm[:, 0:4, tt, :],
                              in_=pf4b[:, 0:512].rearrange("p (h i) -> p h i", h=4))
                return v.tensor_copy(out=k_fm[:, 128 * tt:128 * tt + 128],
                                     in_=pf4b[:, 512:640])
            fm_tok = b.step("vector", f_v9, waits=[qkt_tok], inc=a_v)
            pb1_free = fm_tok

            b.step("tensor", lambda t: t.nop(), waits=[qkt_tok], inc=a_fin)
            b.step("vector", lambda v: v.nop(), waits=[fm_tok], inc=a_fin)
            b.step("scalar", lambda s: s.nop(), waits=[qstd_tok, sq_tok], inc=a_fin)
            
    if stage.startswith("An:"):
        phaseA(int(stage.split(":")[1]), cut)
        barrier()
        _dbg_dump(b, [("d_cos", cos_tab[:, :, :], [128, NTT, 128], BF16),
                      ("d_sin", sin_tab[:, :, :], [128, NTT, 128], BF16),
                      ("d_if", invfreq[:, :], [128, 64], F32),
                      ("d_tcol", tcol[:, :], [128, NTT], F32),
                      ("d_r2a", r2_s[:, 0, :], [128, 640], F32),
                      ("d_r2b", r2_s[:, 1, :], [128, 640], F32),
                      ("d_r1a", r1_s[:, 0, :], [128, 640], F32),
                      ("d_r1b", r1_s[:, 1, :], [128, 640], F32)])
        return _finish(b, nc)
    phaseA()
    if stage == "A":
        barrier()
        _dbg_dump(b, [("d_qfm", q_fm[:, 0, :, :], [128, NTT * 128], BF16),
                      ("d_qfm3", q_fm[:, 3, :, :], [128, NTT * 128], BF16),
                      ("d_kfm", k_fm[:, :], [128, NTT * 128], BF16),
                      ("d_vtm", v_tm[:, :, :], [128, NTT * 128], BF16),
                      ("d_cos", cos_tab[:, :, :], [128, NTT * 128], BF16),
                      ("d_sin", sin_tab[:, :, :], [128, NTT * 128], BF16)])
        return _finish(b, nc)
    barrier()

    def phaseB(nunits=4 * NTT, bcut=9):
        bar_tok = (bar, bar.n)
        s1_free = {0: bar_tok, 1: bar_tok}
        s2_free = {0: bar_tok, 1: bar_tok}
        ptb_free = {0: bar_tok, 1: bar_tok}
        p_free = {0: bar_tok, 1: bar_tok}
        pt_free = {0: bar_tok, 1: bar_tok}
        ab_free = bar_tok
        atr_free = bar_tok

        def emit_pass1(u):
            h, qb = u // NTT, u % NTT
            nkt = qb // 4 + 1
            kws = [512] * (nkt - 1) + [128 * (qb % 4) + 128]
            us = u % 4
            blanket = (b_fin, 3 * (u - 3)) if u >= 4 else None
            bl = [blanket] if blanket else []

            for kt in range(nkt):
                kw = kws[kt]
                diag = kt == nkt - 1

                def f_s1mm(t, h=h, qb=qb, kt=kt, kw=kw):
                    return t.matmul(out=pf[kt % 2][:, 0:kw], lhsT=q_fm[:, h, qb, :],
                                    rhs=k_fm[:, 512 * kt:512 * kt + kw],
                                    start=True, stop=True)
                s1_tok = b.step("tensor", f_s1mm, waits=bl + [s1_free[kt % 2]], inc=b_t)

                def f_mx(v, kt=kt, kw=kw, diag=diag, us=us):
                    pk = pf[kt % 2]
                    if diag:
                        v.tensor_tensor(out=pk[:, kw - 128:kw], in0=pk[:, kw - 128:kw],
                                        in1=tri[:, :], op=ALU.add)
                        v.drain()
                    return v.reduce_max(out=mcol[:, us, kt:kt + 1], in_=pk[:, 0:kw],
                                        axis=AX)
                tok = b.step("vector", f_mx, waits=bl + [s1_tok], inc=b_v)
                s1_free[kt % 2] = tok

            def f_m(v, us=us, nkt=nkt):
                v.reduce_max(out=mrow[:, us, 2:3], in_=mcol[:, us, 0:nkt], axis=AX)
                v.drain()
                return v.tensor_scalar_mul(out=mrow[:, us, 0:1], in0=mrow[:, us, 2:3],
                                           scalar1=-ISD)
            m_tok = b.step("vector", f_m, inc=b_v)
            return dict(h=h, qb=qb, nkt=nkt, kws=kws, us=us, bl=bl, m_tok=m_tok)

        def emit_pass2(st):
            h, qb, nkt, kws, us, bl, m_tok = (st[k] for k in
                                              ("h", "qb", "nkt", "kws", "us", "bl", "m_tok"))
            nonlocal ab_free, atr_free
            v_last = None
            pend = []

            def emit_trv(kt, kw, ks, exp_tok, nch):
                nonlocal v_last

                def f_tr(t, kw=kw, ks=ks, nch=nch):
                    for ci in range(nch):
                        inst = t.transpose(out=pbank[ks][:, 128 * ci:128 * ci + 128],
                                           in_=p_sb[:, ks, 128 * ci:128 * ci + 128],
                                           identity=ident[:, :])
                    return inst
                tr_tok = b.step("tensor", f_tr, waits=[exp_tok, ptb_free[ks]], inc=b_t)
                p_free[ks] = tr_tok

                if kt % 2 == 0:
                    def f_ev(v, kw=kw, ks=ks):
                        return v.tensor_copy(out=pt_sb[:, ks, 0:kw],
                                             in_=pbank[ks][:, 0:kw])
                    ev_tok = b.step("vector", f_ev, waits=[tr_tok, pt_free[ks]], inc=b_v)
                else:
                    def f_ev(sc, kw=kw, ks=ks):
                        return sc.activation(out=pt_sb[:, ks, 0:kw],
                                             in_=pbank[ks][:, 0:kw],
                                             func=AF.Copy)
                    ev_tok = b.step("scalar", f_ev, waits=[tr_tok, pt_free[ks]], inc=b_s)
                ptb_free[ks] = ev_tok

                vw = [ev_tok]
                if kt == 0:
                    vw.append(ab_free)

                def f_vmm(t, kt=kt, ks=ks, nch=nch, nkt=nkt):
                    for ci in range(nch):
                        inst = t.matmul(out=pf[4][:, 0:128],
                                        lhsT=pt_sb[:, ks, 128 * ci:128 * ci + 128],
                                        rhs=v_tm[:, 4 * kt + ci, :],
                                        start=(kt == 0 and ci == 0),
                                        stop=(kt == nkt - 1 and ci == nch - 1))
                    return inst
                v_tok = b.step("tensor", f_vmm, waits=vw, inc=b_t)
                pt_free[ks] = v_tok
                v_last = v_tok

            for kt in range(nkt):
                kw = kws[kt]
                diag = kt == nkt - 1
                ks = kt % 2

                def f_s2mm(t, h=h, qb=qb, kt=kt, kw=kw, ks=ks):
                    return t.matmul(out=pf[2 + ks][:, 0:kw], lhsT=q_fm[:, h, qb, :],
                                    rhs=k_fm[:, 512 * kt:512 * kt + kw],
                                    start=True, stop=True)
                s2_tok = b.step("tensor", f_s2mm, waits=bl + [s2_free[ks]], inc=b_t)

                exp_w = [s2_tok, m_tok, p_free[ks]]
                if diag:
                    def f_mk2(v, kw=kw, ks=ks):
                        pk = pf[2 + ks]
                        return v.tensor_tensor(out=pk[:, kw - 128:kw],
                                               in0=pk[:, kw - 128:kw],
                                               in1=tri[:, :], op=ALU.add)
                    mk2_tok = b.step("vector", f_mk2, waits=[s2_tok], inc=b_v)
                    exp_w.append(mk2_tok)

                def f_exp(sc, kt=kt, kw=kw, ks=ks, us=us):
                    return sc.activation(out=p_sb[:, ks, 0:kw], in_=pf[2 + ks][:, 0:kw],
                                         func=AF.Exp, scale=cst[:, 1:2],
                                         bias=mrow[:, us, 0:1],
                                         accum_out=lcol[:, us, kt:kt + 1])
                exp_tok = b.step("scalar", f_exp, waits=exp_w, inc=b_s)
                s2_free[ks] = exp_tok

                pend.append((kt, kw, ks, exp_tok, kw // 128))
                if len(pend) >= 2:
                    emit_trv(*pend.pop(0))
            while pend:
                emit_trv(*pend.pop(0))

            def f_l(v, us=us, nkt=nkt):
                v.reduce_sum(out=mrow[:, us, 3:4], in_=lcol[:, us, 0:nkt], axis=AX)
                v.drain()
                return v.reciprocal(out=mrow[:, us, 1:2], in_=mrow[:, us, 3:4])
            l_tok = b.step("vector", f_l, waits=[(b_s, b_s.n)], inc=b_v)

            def f_ae(v, us=us):
                return v.tensor_scalar_mul(out=atm[:, us, :], in0=pf[4][:, 0:128],
                                           scalar1=mrow[:, us, 1:2])
            ae_tok = b.step("vector", f_ae, waits=[l_tok, v_last], inc=b_v)
            ab_free = ae_tok

            def f_at(t, us=us):
                return t.transpose(out=pf5b[:, 0:128], in_=atm[:, us, :],
                                   identity=ident[:, :])
            at_tok = b.step("tensor", f_at, waits=[ae_tok, atr_free], inc=b_t)

            def f_af(v, h=h, qb=qb):
                return v.tensor_copy(out=attn_fm[:, h, qb, :], in_=pf5b[:, 0:128])
            af_tok = b.step("vector", f_af, waits=[at_tok], inc=b_v)
            atr_free = af_tok

            b.step("tensor", lambda t: t.nop(), waits=[at_tok], inc=b_fin)
            b.step("vector", lambda v: v.nop(), waits=[af_tok], inc=b_fin)
            b.step("scalar", lambda s: s.nop(), waits=[(b_s, b_s.n)], inc=b_fin)

        prev = None
        for u in range(nunits):
            st = emit_pass1(u)
            if prev is not None:
                emit_pass2(prev)
            prev = st
        if prev is not None:
            emit_pass2(prev)

    if stage.startswith("Bn:"):
        parts = stage.split(":")
        phaseB(int(parts[1]), int(parts[2]) if len(parts) > 2 else 9)
        barrier()
        _dbg_dump(b, [("d_attn", attn_fm[:, 0, :, :], [128, NTT * 128], BF16)])
        return _finish(b, nc)
    phaseB()
    if stage == "B":
        barrier()
        _dbg_dump(b, [("d_attn", attn_fm[:, 0, :, :], [128, NTT * 128], BF16),
                      ("d_attn3", attn_fm[:, 3, :, :], [128, NTT * 128], BF16)])
        return _finish(b, nc)
    barrier()

    def phaseB2():
        bar_tok = (bar, bar.n)
        bank_free = {0: bar_tok, 1: bar_tok}
        po_free = {0: bar_tok, 1: bar_tok}
        for i, (os_, tt) in enumerate([(a, t) for a in range(4) for t in range(NTT)]):
            sl = i % 2

            def f_mm(t, os_=os_, tt=tt, sl=sl):
                for fc in range(4):
                    inst = t.matmul(out=pf[sl][:, :], lhsT=attn_fm[:, fc, tt, :],
                                    rhs=wo_sb[:, fc, 512 * os_:512 * os_ + 512],
                                    start=(fc == 0), stop=(fc == 3))
                return inst
            mm_tok = b.step("tensor", f_mm, waits=[bank_free[sl]], inc=c_t)

            if i % 2 == 0:
                def f_ev(v, sl=sl):
                    return v.tensor_copy(out=po_sb[:, sl, :], in_=pf[sl][:, :])
                ev_tok = b.step("vector", f_ev, waits=[mm_tok, po_free[sl]], inc=c_v)
            else:
                def f_ev(s, sl=sl):
                    return s.activation(out=po_sb[:, sl, :], in_=pf[sl][:, :],
                                        func=AF.Copy)
                ev_tok = b.step("scalar", f_ev, waits=[mm_tok, po_free[sl]], inc=c_s)
            bank_free[sl] = ev_tok

            def f_st(s, os_=os_, tt=tt, sl=sl):
                return s.dma_start(out=ccin_hbm[128 * tt:128 * tt + 128,
                                                512 * os_:512 * os_ + 512],
                                   in_=po_sb[:, sl, :])
            st_tok = b.step("sync", f_st, waits=[ev_tok],
                            inc=(c_st0 if sl == 0 else c_st1), amt=16)
            po_free[sl] = st_tok

        def f_cc(g):
            return g.collective_compute(
                "ReduceScatter", ALU.add, replica_groups=rg,
                ins=[ccin_hbm[:, :]], outs=[ccout_hbm[:, :]])
        b.step("gpsimd", f_cc, waits=[(c_st0, c_st0.n), (c_st1, c_st1.n)],
               inc=cc_done)

    phaseB2()
    barrier()
    cc_tok = (cc_done, cc_done.n)

    def phaseD(d0_only=False):
        bar_tok = (bar, bar.n)

        def f_sc(s, sem):
            s.dma_start(out=sg_fm[:, :],
                        in_=sg_hbm.rearrange("o (c p) -> p (o c)", p=128),
                        allow_slow_non_contiguous=True).then_inc(sem, 16)
            s.dma_start(out=su_fm[:, :],
                        in_=su_hbm.rearrange("o (c p) -> p (o c)", p=128),
                        allow_slow_non_contiguous=True).then_inc(sem, 16)
        b.step("sync", f_sc, inc=d_sc, amt=2 * 16, self_inc=True)
        sc_tok = (d_sc, d_sc.n)

        pb0_free = {0: bar_tok, 1: bar_tok}
        x_free = bar_tok
        xn_toks = []
        for ot in range(8):
            bf = ot % 2

            def f_x(s, ot=ot):
                src = xown_hbm.rearrange("(n p) d -> p n d", p=128)
                return s.dma_start(out=xo_sb[:, :], in_=src[:, ot, :])
            x_tok = b.step("sync", f_x, waits=[x_free], inc=d_xo, amt=16)

            def f_v1(v, bf=bf):
                return v.reduce_sum(out=st_d[:, bf, 0:1], in_=xo_sb[:, :], axis=AX)
            sum_tok = b.step("vector", f_v1, waits=[x_tok], inc=d_v)

            def f_s1(s, bf=bf):
                return s.activation(out=sqd_d[:, :], in_=xo_sb[:, :], func=AF.Square,
                                    accum_out=st_d[:, bf, 1:2])
            ssq_tok = b.step("scalar", f_s1, waits=[x_tok], inc=d_s)

            def f_v2(v, bf=bf):
                stx = st_d[:, bf, :]
                v.tensor_scalar_mul(out=stx[:, 2:3], in0=stx[:, 0:1], scalar1=1.0 / HID)
                v.drain()
                v.tensor_tensor(out=stx[:, 3:4], in0=stx[:, 2:3], in1=stx[:, 2:3],
                                op=ALU.mult)
                v.drain()
                return v.scalar_tensor_tensor(out=stx[:, 4:5], in0=stx[:, 1:2],
                                              scalar=1.0 / HID, in1=stx[:, 3:4],
                                              op0=ALU.mult, op1=ALU.subtract)
            var_tok = b.step("vector", f_v2, waits=[ssq_tok, sum_tok], inc=d_v)

            def f_s2(s, bf=bf):
                return s.activation(out=st_d[:, bf, 5:6], in_=st_d[:, bf, 4:5],
                                    func=AF.Sqrt, bias=cst[:, 0:1])
            std_tok = b.step("scalar", f_s2, waits=[var_tok], inc=d_s)

            def f_v3(v, bf=bf):
                v.reciprocal(out=st_d[:, bf, 6:7], in_=st_d[:, bf, 5:6])
                v.drain()
                return v.tensor_scalar(out=xno_tm[:, bf, :], in0=xo_sb[:, :],
                                       scalar1=st_d[:, bf, 2:3],
                                       scalar2=st_d[:, bf, 6:7],
                                       op0=ALU.subtract, op1=ALU.mult)
            xn_tok = b.step("vector", f_v3, waits=[std_tok, ssq_tok], inc=d_v)
            x_free = xn_tok

            for grp in range(4):
                half = grp % 2

                def f_tr(t, bf=bf, grp=grp, half=half):
                    for j in range(4):
                        dc = grp * 4 + j
                        inst = t.transpose(out=pbank[half][:, 128 * j:128 * j + 128],
                                           in_=xno_tm[:, bf, 128 * dc:128 * dc + 128],
                                           identity=ident[:, :])
                    return inst
                tr_tok = b.step("tensor", f_tr, waits=[xn_tok, pb0_free[half]], inc=d_t)

                def f_ev(v, ot=ot, grp=grp, half=half):
                    return v.tensor_copy(
                        out=xn_own[:, grp * 4:(grp + 1) * 4, 128 * ot:128 * ot + 128],
                        in_=pbank[half][:, 0:512].rearrange("p (c i) -> p c i", c=4))
                ev_tok = b.step("vector", f_ev, waits=[tr_tok], inc=d_v)
                pb0_free[half] = ev_tok
                xn_toks.append(ev_tok)
        xn_done = xn_toks[-1]
        barrier()
        xn_done = (bar, bar.n)
        if d0_only:
            return

        wld_free = {0: [bar_tok], 1: [bar_tok]}
        wcast_free = {0: [bar_tok], 1: [bar_tok]}
        gs_free = {0: bar_tok, 1: bar_tok}
        up_free = {0: bar_tok, 1: bar_tok}
        bank_free = {id(pbf[0]): pb0_free[0], id(pbf[1]): pb0_free[1]}
        gu_last = None
        for hc in range(NHC):
            sl = hc % 2

            def f_wl(s, hc=hc):
                return s.dma_start(out=wgI[:, :, :],
                                   in_=wg_hbm.rearrange("(c p) f -> p c f", p=128
                                                        )[:, :, 128 * hc:128 * hc + 128])
            wl_tok = b.step("sync", f_wl, waits=wld_free[0], inc=d_wgu, amt=16)

            def f_wl2(sc, hc=hc):
                return sc.dma_start(out=wuI[:, :, :],
                                    in_=wu_hbm.rearrange("(c p) f -> p c f", p=128
                                                         )[:, :, 128 * hc:128 * hc + 128])
            wl2_tok = b.step("scalar", f_wl2, waits=wld_free[0], inc=d_wgu2, amt=16)

            def f_cg(v, sl=sl):
                return v.tensor_scalar_add(out=wg_sb[:, sl, :, :], in0=wgI[:, :, :],
                                           scalar1=-128.0)
            cg_tok = b.step("vector", f_cg, waits=[wl_tok] + wcast_free[sl], inc=d_v)

            def f_cu(s, sl=sl):
                return s.activation(out=wu_sb[:, sl, :, :], in_=wuI[:, :, :],
                                    func=AF.Copy, bias=-128.0)
            cu_tok = b.step("scalar", f_cu, waits=[wl2_tok] + wcast_free[sl], inc=d_s)
            wld_free[0] = [cg_tok, cu_tok]

            banks = [pf[0], pf[1], pf[2], pf[3]] if sl == 0 else \
                    [pf[4], pf[5], pbf[0], pbf[1]]
            for tsl in range(2):
                bg, bu = banks[tsl], banks[2 + tsl]

                def f_mm(t, sl=sl, tsl=tsl, bg=bg, bu=bu):
                    for dc in range(NDC):
                        t.matmul(out=bg[:, :], lhsT=wg_sb[:, sl, dc, :],
                                 rhs=xn_own[:, dc, 512 * tsl:512 * tsl + 512],
                                 start=(dc == 0), stop=(dc == NDC - 1))
                        inst = t.matmul(out=bu[:, :], lhsT=wu_sb[:, sl, dc, :],
                                        rhs=xn_own[:, dc, 512 * tsl:512 * tsl + 512],
                                        start=(dc == 0), stop=(dc == NDC - 1))
                    return inst
                mmw = [cg_tok, cu_tok, xn_done]
                for bk in (id(bg), id(bu)):
                    if bk in bank_free:
                        mmw.append(bank_free[bk])
                mm_tok = b.step("tensor", f_mm, waits=mmw, inc=d_t)

                def f_sil(s, hc=hc, tsl=tsl, bg=bg):
                    return s.activation(out=gs_tmp[:, tsl, :], in_=bg[:, :],
                                        func=AF.Silu, scale=sg_fm[:, hc:hc + 1])
                sil_tok = b.step("scalar", f_sil,
                                 waits=[mm_tok, sc_tok, gs_free[tsl]], inc=d_s)

                def f_up(v, hc=hc, tsl=tsl, bu=bu):
                    return v.tensor_scalar_mul(out=up_tmp[:, tsl, :], in0=bu[:, :],
                                               scalar1=su_fm[:, hc:hc + 1])
                up_tok = b.step("vector", f_up,
                                waits=[mm_tok, sc_tok, up_free[tsl]], inc=d_v)

                def f_gu(v, hc=hc, tsl=tsl):
                    return v.tensor_mul(out=gu_fm[:, hc, 512 * tsl:512 * tsl + 512],
                                        in0=gs_tmp[:, tsl, :], in1=up_tmp[:, tsl, :])
                gu_tok = b.step("vector", f_gu, waits=[sil_tok, up_tok], inc=d_v)
                bank_free[id(bg)] = gu_tok
                bank_free[id(bu)] = gu_tok
                gs_free[tsl] = gu_tok
                up_free[tsl] = gu_tok
                gu_last = gu_tok
            wcast_free[sl] = [mm_tok]

        wd_free = {0: (d_t, 0), 1: (d_t, 0)}
        wdc_free = {0: (d_t, 0), 1: (d_t, 0)}
        dbank_free = {i: gu_last for i in range(4)}
        rs_free = {0: (d_v, 0), 1: (d_v, 0)}
        fin_free = {0: (d_v, 0), 1: (d_v, 0)}
        i = 0
        sd_free = (d_v, 0)
        for os_ in range(4):
            def f_sd(s, os_=os_):
                return s.dma_start(out=sd_rep[:, :],
                                   in_=sd_hbm[:, 512 * os_:512 * os_ + 512
                                              ].broadcast_to((128, 512)))
            sd_tok = b.step("sync", f_sd, waits=[sd_free], inc=d_sd, amt=16)
            for tb in range(2):
                mm_toks = {t4: None for t4 in range(4)}
                for hc in range(NHC):
                    sl = hc % 2

                    def f_wd(g, hc=hc, os_=os_, sl=sl):
                        return g.dma_start(
                            out=wd_sb[:, sl, :],
                            in_=wd_hbm[128 * hc:128 * hc + 128,
                                       512 * os_:512 * os_ + 512])
                    wdl_tok = b.step("gpsimd", f_wd,
                                     waits=[wd_free[sl], wdc_free[sl]],
                                     inc=(d_wd0 if sl == 0 else d_wd1), amt=16)

                    def f_cd(v, sl=sl):
                        return v.tensor_scalar_add(out=wd_sb[:, sl, :],
                                                   in0=wd_sb[:, sl, :], scalar1=-128.0)
                    cd_tok = b.step("vector", f_cd, waits=[wdl_tok], inc=d_v)

                    def f_mm(t, hc=hc, tb=tb, sl=sl):
                        for t4 in range(4):
                            inst = t.matmul(
                                out=pf[t4][:, :],
                                lhsT=gu_fm[:, hc, 512 * tb + 128 * t4:
                                           512 * tb + 128 * t4 + 128],
                                rhs=wd_sb[:, sl, :],
                                start=(hc == 0), stop=(hc == NHC - 1))
                        return inst
                    mmw = [cd_tok, gu_last]
                    if hc == 0:
                        mmw += [dbank_free[t4] for t4 in range(4)]
                    mm_tok = b.step("tensor", f_mm, waits=mmw, inc=d_t)
                    wd_free[sl] = mm_tok
                    wdc_free[sl] = mm_tok
                    for t4 in range(4):
                        mm_toks[t4] = mm_tok

                for t4 in range(4):
                    sl = i % 2
                    row0 = 512 * tb + 128 * t4

                    def f_ld(s, sem, row0=row0, os_=os_, sl=sl):
                        s.dma_start(out=rs_sb[:, sl, :],
                                    in_=ccout_hbm[row0:row0 + 128,
                                                  512 * os_:512 * os_ + 512]
                                    ).then_inc(sem, 16)
                        s.dma_start(out=xot_sb[:, sl, :],
                                    in_=xown_hbm[row0:row0 + 128,
                                                 512 * os_:512 * os_ + 512]
                                    ).then_inc(sem, 16)
                    ld_tok = b.step("sync", f_ld, waits=[cc_tok, rs_free[sl]],
                                    inc=(d_rs0 if sl == 0 else d_rs1), amt=32,
                                    self_inc=True)

                    def f_f1(v, t4=t4, sl=sl):
                        v.tensor_mul(out=out_sb[:, sl, :], in0=pf[t4][:, :],
                                     in1=sd_rep[:, :])
                        v.drain()
                        v.tensor_add(out=out_sb[:, sl, :], in0=out_sb[:, sl, :],
                                     in1=xot_sb[:, sl, :])
                        v.drain()
                        return v.tensor_add(out=out_sb[:, sl, :], in0=out_sb[:, sl, :],
                                            in1=rs_sb[:, sl, :])
                    f1_tok = b.step("vector", f_f1,
                                    waits=[mm_toks[t4], ld_tok, sd_tok, fin_free[sl]],
                                    inc=d_v)
                    dbank_free[t4] = f1_tok
                    rs_free[sl] = f1_tok

                    def f_out(s, row0=row0, os_=os_, sl=sl):
                        return s.dma_start(out=out_hbm[row0:row0 + 128,
                                                       512 * os_:512 * os_ + 512],
                                           in_=out_sb[:, sl, :])
                    out_tok = b.step("sync", f_out, waits=[f1_tok],
                                     inc=(o_st0 if sl == 0 else o_st1), amt=16)
                    fin_free[sl] = out_tok
                    sd_free = f1_tok
                    i += 1

        b.step("sync", lambda s: s.nop(),
               waits=[(o_st0, o_st0.n), (o_st1, o_st1.n)])

    if stage == "D0d":
        phaseD(d0_only=True)
        _dbg_dump(b, [("d_xn", xn_own[:, :, :], [128, NDC, OWN], BF16)])
        return _finish(b, nc)
    phaseD()
    if stage == "Dd":
        _dbg_dump(b, [("d_xn", xn_own[:, :, :], [128, NDC, OWN], BF16),
                      ("d_gu", gu_fm[:, :, :], [128, NHC, OWN], BF16),
                      ("d_rs", ccout_hbm[:, :], [OWN, HID], F32),
                      ("d_ci", ccin_hbm[:, :], [S, HID], F32)])
    return _finish(b, nc)


def _finish(b, nc):
    blk = nc.Block()
    block = blk.__enter__()
    b._ctxs.append(blk)

    @block.tensor
    def _(eng):
        b.replay("tensor", eng)

    @block.vector
    def _(eng):
        b.replay("vector", eng)

    @block.scalar
    def _(eng):
        b.replay("scalar", eng)

    @block.gpsimd
    def _(eng):
        b.replay("gpsimd", eng)

    @block.sync
    def _(eng):
        b.replay("sync", eng)

    blk.__exit__(None, None, None)
    b._ctxs.pop()
    return b


def _dbg_dump(b, specs):
    nc = b.nc
    dsem = b.sem("dbg_st")
    for name, ap, shape, dtype in specs:
        hbm = nc.dram_tensor(name, shape, dtype, kind="ExternalOutput").ap()
        b.dbg_outs.append(name)

        def f(s, sem, hbm=hbm, ap=ap):
            s.dma_start(out=hbm, in_=ap).then_inc(sem, 16)
        b.step("sync", f, waits=[], inc=dsem, amt=16, self_inc=True)
    b.step("sync", lambda s: s.nop(), waits=[(dsem, dsem.n)])


_BUILD_CACHE = {}


def _get_build(dbg=(), stage="full"):
    key = (tuple(sorted(dbg)), stage)
    if key not in _BUILD_CACHE:
        _BUILD_CACHE[key] = build(dbg, stage)
    return _BUILD_CACHE[key]


def make_in_maps(inputs):
    x = np.ascontiguousarray(np.asarray(inputs["L0_attn_x"], dtype=np.float32))
    w_qkv = np.asarray(inputs["L0_attn_w_qkv"], dtype=np.float32)
    w_o = np.asarray(inputs["L0_attn_w_o"], dtype=np.float32)
    qn_g = np.asarray(inputs["L0_attn_qn_g"], dtype=np.float32).reshape(1, D)
    qn_b = np.asarray(inputs["L0_attn_qn_b"], dtype=np.float32).reshape(1, D)
    kn_g = np.asarray(inputs["L0_attn_kn_g"], dtype=np.float32).reshape(1, D)
    kn_b = np.asarray(inputs["L0_attn_kn_b"], dtype=np.float32).reshape(1, D)
    wg_t = np.ascontiguousarray(np.asarray(inputs["L0_ffn_wq_gate"], dtype=np.int32).T)
    wu_t = np.ascontiguousarray(np.asarray(inputs["L0_ffn_wq_up"], dtype=np.int32).T)
    wd_t = np.ascontiguousarray(np.asarray(inputs["L0_ffn_wq_down"], dtype=np.int32).T)
    s_gate = np.asarray(inputs["L0_ffn_s_gate"], dtype=np.float32).reshape(1, FH)
    s_up = np.asarray(inputs["L0_ffn_s_up"], dtype=np.float32).reshape(1, FH)
    s_down = np.asarray(inputs["L0_ffn_s_down"], dtype=np.float32).reshape(1, HID)

    import ml_dtypes
    ident = np.eye(128, dtype=np.float32).astype(ml_dtypes.bfloat16)
    iot64 = np.arange(64, dtype=np.float32).reshape(1, 64)
    tcol_h = (np.arange(128, dtype=np.float32)[:, None]
              + 128.0 * np.arange(NTT, dtype=np.float32)[None, :])
    tcol_h = np.ascontiguousarray(tcol_h)
    tri = np.where(np.arange(128)[None, :] > np.arange(128)[:, None],
                   np.float32(NEG_BIG), np.float32(0.0)).astype(np.float32)

    wqkv_T = np.ascontiguousarray(w_qkv.T)
    wo_T = np.ascontiguousarray(w_o.T)

    in_maps = []
    for c in range(8):
        g, r = c // 4, c % 4
        cols = np.concatenate([
            np.arange(512 * r, 512 * r + 512),
            np.arange(2048 + 128 * r, 2048 + 128 * r + 128),
            np.arange(2560 + 128 * r, 2560 + 128 * r + 128),
        ])
        wqkv_slice = np.ascontiguousarray(wqkv_T[:, cols])
        wo_slice = np.ascontiguousarray(wo_T[512 * r:512 * r + 512, :])
        in_maps.append({
            "x": x[g],
            "x_own": np.ascontiguousarray(x[g, 1024 * r:1024 * (r + 1)]),
            "wqkv_t": wqkv_slice,
            "wo_t": wo_slice,
            "qn_g": qn_g, "qn_b": qn_b, "kn_g": kn_g, "kn_b": kn_b,
            "wg_t": wg_t, "wu_t": wu_t, "wd_t": wd_t,
            "s_gate": s_gate, "s_up": s_up, "s_down": s_down,
            "tri": tri, "ident": ident, "iot64": iot64, "tcol_h": tcol_h,
        })
    return in_maps


def kernel(**inputs):
    b = _get_build()
    in_maps = make_in_maps(inputs)
    res = run_bass_kernel_spmd(b.nc, in_maps, core_ids=list(range(8)))
    out = np.empty((2, S, HID), dtype=np.float32)
    for c in range(8):
        g, r = c // 4, c % 4
        out[g, 1024 * r:1024 * (r + 1)] = res.results[c]["out"]
    return out


# version 05
# speedup vs baseline: 1.1644x; 1.1532x over previous
import math

import numpy as np
import concourse.bass as bass
import concourse.mybir as mybir
from concourse.bass_utils import run_bass_kernel_spmd
from concourse.replica_groups import maybe_share_collective_output_space

F32 = mybir.dt.float32
BF16 = mybir.dt.bfloat16
I32 = mybir.dt.int32
AF = mybir.ActivationFunctionType
ALU = mybir.AluOpType
AX = mybir.AxisListType.X

S = 4096
HID = 2048
NH = 4
D = 128
FH = 5632
NTT = S // 128
NHC = FH // 128
NDC = HID // 128
OWN = 1024

ISD = 1.0 / math.sqrt(D)
TWO_PI = 2.0 * math.pi
C_EPS = 1e-5
C_PI2 = math.pi / 2.0
C_NEG1 = -1.0
C_IVF = -2.0 * math.log(1.0e6) / D
NEG_BIG = -1.0e30

ENGS = ("tensor", "vector", "scalar", "gpsimd", "sync")


class Sem:
    __slots__ = ("h", "n", "name")

    def __init__(self, h, name):
        self.h = h
        self.n = 0
        self.name = name


class B:

    def __init__(self, dbg=()):
        self.dbg = set(dbg)
        self.nc = bass.Bass("TRN2", target_bir_lowering=False, detect_race_conditions=False)
        self.steps = {e: [] for e in ENGS}
        self._ctxs = []
        self.sems = {}
        self.dbg_outs = []

    def sem(self, name):
        if name not in self.sems:
            g = self.nc.semaphore(name)
            h = g.__enter__()
            self._ctxs.append(g)
            self.sems[name] = Sem(h, name)
        return self.sems[name]

    def sbuf(self, name, shape, dtype):
        g = self.nc.sbuf_tensor(name, shape, dtype)
        t = g.__enter__()
        self._ctxs.append(g)
        return t

    def psum(self, name, shape, dtype):
        g = self.nc.psum_tensor(name, shape, dtype)
        t = g.__enter__()
        self._ctxs.append(g)
        return t

    def step(self, eng, emit, waits=(), inc=None, amt=1, self_inc=False):
        ws = tuple((s, v) for (s, v) in waits if s is not None and v > 0)
        tok = None
        if inc is not None:
            inc.n += amt
            tok = (inc, inc.n)
        self.steps[eng].append((ws, emit, (inc.h if inc else None), amt, self_inc))
        return tok

    def replay(self, eng_name, eng):
        fence = False
        for ws, emit, inch, amt, self_inc in self.steps[eng_name]:
            for (s, v) in ws:
                eng.wait_ge(s.h, v)
            if self_inc:
                emit(eng, inch)
            else:
                inst = emit(eng)
                if inch is not None:
                    assert inst is not None, f"step on {eng_name} with inc returned None"
                    if fence:
                        inst = eng.drain()
                    inst.then_inc(inch, amt)


def build(dbg=(), stage="full", cut=99):
    b = B(dbg)
    nc = b.nc

    x_hbm = nc.dram_tensor("x", [S, HID], F32, kind="ExternalInput").ap()
    xown_hbm = nc.dram_tensor("x_own", [OWN, HID], F32, kind="ExternalInput").ap()
    wqkv_hbm = nc.dram_tensor("wqkv_t", [HID, 768], F32, kind="ExternalInput").ap()
    wo_hbm = nc.dram_tensor("wo_t", [512, HID], F32, kind="ExternalInput").ap()
    qng_hbm = nc.dram_tensor("qn_g", [1, D], F32, kind="ExternalInput").ap()
    qnb_hbm = nc.dram_tensor("qn_b", [1, D], F32, kind="ExternalInput").ap()
    kng_hbm = nc.dram_tensor("kn_g", [1, D], F32, kind="ExternalInput").ap()
    knb_hbm = nc.dram_tensor("kn_b", [1, D], F32, kind="ExternalInput").ap()
    wg_hbm = nc.dram_tensor("wg_t", [HID, FH], I32, kind="ExternalInput").ap()
    wu_hbm = nc.dram_tensor("wu_t", [HID, FH], I32, kind="ExternalInput").ap()
    wd_hbm = nc.dram_tensor("wd_t", [FH, HID], I32, kind="ExternalInput").ap()
    sg_hbm = nc.dram_tensor("s_gate", [1, FH], F32, kind="ExternalInput").ap()
    su_hbm = nc.dram_tensor("s_up", [1, FH], F32, kind="ExternalInput").ap()
    sd_hbm = nc.dram_tensor("s_down", [1, HID], F32, kind="ExternalInput").ap()
    tri_hbm = nc.dram_tensor("tri", [128, 128], F32, kind="ExternalInput").ap()
    iot_hbm = nc.dram_tensor("iot64", [1, 64], F32, kind="ExternalInput").ap()
    tcol_hbm = nc.dram_tensor("tcol_h", [128, NTT], F32, kind="ExternalInput").ap()
    id_hbm = nc.dram_tensor("ident", [128, 128], BF16, kind="ExternalInput").ap()

    out_hbm = nc.dram_tensor("out", [OWN, HID], F32, kind="ExternalOutput").ap()

    rg = [[0, 1, 2, 3], [4, 5, 6, 7]]
    cc_space = maybe_share_collective_output_space("ReduceScatter", rg)
    ccin_hbm = nc.dram_tensor("cc_in", [S, HID], F32, kind="Internal").ap()
    ccout_hbm = nc.dram_tensor("cc_out", [OWN, HID], F32, kind="Internal",
                               addr_space=cc_space).ap()

    ident = b.sbuf("ident_sb", [128, 128], BF16)
    tri = b.sbuf("tri_sb", [128, 128], F32)
    qg_rep = b.sbuf("qg_rep", [128, D], F32)
    qb_rep = b.sbuf("qb_rep", [128, D], F32)
    kg_rep = b.sbuf("kg_rep", [128, D], F32)
    kb_rep = b.sbuf("kb_rep", [128, D], F32)
    cst = b.sbuf("cst", [128, 8], F32)
    invfreq = b.sbuf("invfreq", [128, 64], F32)
    invfreq_i = b.sbuf("invfreq_i", [128, 64], F32)

    ARENA_KB = 200
    arena = b.sbuf("arena", [128, ARENA_KB * 1024], mybir.dt.uint8)
    arena_base = nc.lookup_mloc(arena).addr

    def at(off, name, shape, dtype):
        per_part = int(np.prod(shape[1:])) * mybir.dt.size(dtype)
        off = (off + 31) // 32 * 32
        assert off + per_part <= ARENA_KB * 1024, (name, off, per_part)
        t = nc.alloc_sbuf_tensor_at(name, shape, dtype, offset=arena_base + off)
        return t, off + per_part

    o = 0
    q_fm, o = at(o, "q_fm", [128, NH, NTT, 128], BF16)
    k_fm, o = at(o, "k_fm", [128, NTT * 128], BF16)
    v_tm, o = at(o, "v_tm", [128, NTT, 128], BF16)
    attn_fm, o = at(o, "attn_fm", [128, NH, NTT, 128], BF16)
    wqkv_sb, o = at(o, "wqkv_sb", [128, NDC, 768], BF16)
    wo_sb, o = at(o, "wo_sb", [128, 4, HID], BF16)
    AB_END = o

    o = AB_END
    x_sb, o = at(o, "x_sb", [128, 2, HID], F32)
    xn_tm, o = at(o, "xn_tm", [128, 2, HID], BF16)
    xn_fm, o = at(o, "xn_fm", [128, 2, NDC, 128], BF16)
    cos_tab, o = at(o, "cos_tab", [128, NTT, D], BF16)
    sin_tab, o = at(o, "sin_tab", [128, NTT, D], BF16)
    tcol, o = at(o, "tcol", [128, NTT], F32)
    sqd, o = at(o, "sqd", [128, 2, 640], F32)
    st_a, o = at(o, "st_a", [128, 2, 40], F32)
    xc_s, o = at(o, "xc_s", [128, 2, 640], F32)
    y_s, o = at(o, "y_s", [128, 2, 640], F32)
    r1_s, o = at(o, "r1_s", [128, 2, 640], F32)
    r2_s, o = at(o, "r2_s", [128, 2, 640], F32)
    qk_tm, o = at(o, "qk_tm", [128, 2, 640], BF16)
    sqx, o = at(o, "sqx", [128, HID], BF16)
    assert o <= ARENA_KB * 1024

    o = AB_END
    p_sb, o = at(o, "p_sb", [128, 2, 512], BF16)
    pt_sb, o = at(o, "pt_sb", [128, 2, 512], BF16)
    mcol, o = at(o, "mcol", [128, 4, 8], F32)
    lcol, o = at(o, "lcol", [128, 4, 8], F32)
    mrow, o = at(o, "mrow", [128, 4, 4], F32)
    atm, o = at(o, "atm", [128, 4, 128], BF16)
    po_sb, o = at(o, "po_sb", [128, 2, 512], F32)
    assert o <= ARENA_KB * 1024

    o = 0
    xn_own, o = at(o, "xn_own", [128, NDC, OWN], BF16)
    gu_fm, o = at(o, "gu_fm", [128, NHC, OWN], BF16)
    wgI, o = at(o, "wgI", [128, NDC, 128], I32)
    wuI, o = at(o, "wuI", [128, NDC, 128], I32)
    wg_sb, o = at(o, "wg_sb", [128, 2, NDC, 128], BF16)
    wu_sb, o = at(o, "wu_sb", [128, 2, NDC, 128], BF16)
    wd_sb, o = at(o, "wd_sb", [128, 2, 4, 512], BF16)
    sd_rep, o = at(o, "sd_rep", [128, 512], F32)
    sg_fm, o = at(o, "sg_fm", [128, NHC], F32)
    su_fm, o = at(o, "su_fm", [128, NHC], F32)
    xo_sb, o = at(o, "xo_sb", [128, HID], F32)
    xno_tm, o = at(o, "xno_tm", [128, 2, HID], BF16)
    gs_tmp, o = at(o, "gs_tmp", [128, 2, 512], BF16)
    up_tmp, o = at(o, "up_tmp", [128, 2, 512], BF16)
    rs_sb, o = at(o, "rs_sb", [128, 2, 512], F32)
    xot_sb, o = at(o, "xot_sb", [128, 2, 512], F32)
    out_sb, o = at(o, "out_sb", [128, 2, 512], F32)
    st_d, o = at(o, "st_d", [128, 2, 8], F32)
    sqd_d, o = at(o, "sqd_d", [128, HID], BF16)
    assert o <= ARENA_KB * 1024

    pf = [b.psum(f"pf{i}", [128, 512], F32) for i in range(6)]
    pbank = [b.psum(f"pb{i}", [128, 1024], BF16) for i in range(2)]
    pbf = [pbank[i][:, :].bitcast(F32) for i in range(2)]
    pf4b = pf[4][:, :].bitcast(BF16)
    pf5b = pf[5][:, :].bitcast(BF16)

    for i, v in enumerate((C_EPS, ISD, C_PI2, C_NEG1, C_IVF)):
        nc.const_aps.register(F32, v, cst[:, i:i + 1])

    s_ld = b.sem("s_ld")
    s_iot = b.sem("s_iot")
    g_ld = b.sem("g_ld")
    a_x0 = b.sem("a_x0")
    a_x1 = b.sem("a_x1")
    bar = b.sem("bar")
    a_fin = b.sem("a_fin")
    a_v = b.sem("a_v")
    a_s = b.sem("a_s")
    a_g = b.sem("a_g")
    a_t = b.sem("a_t")
    b_fin = b.sem("b_fin")
    b_t = b.sem("b_t")
    b_v = b.sem("b_v")
    b_s = b.sem("b_s")
    c_t = b.sem("c_t")
    c_v = b.sem("c_v")
    c_s = b.sem("c_s")
    c_st0 = b.sem("c_st0")
    c_st1 = b.sem("c_st1")
    cc_done = b.sem("cc_done")
    d_sc = b.sem("d_sc")
    d_xo = b.sem("d_xo")
    d_wgu = b.sem("d_wgu")
    d_wgu2 = b.sem("d_wgu2")
    d_wd0 = b.sem("d_wd0")
    d_wd1 = b.sem("d_wd1")
    d_rs0 = b.sem("d_rs0")
    d_rs1 = b.sem("d_rs1")
    d_sd = b.sem("d_sd")
    d_v = b.sem("d_v")
    d_s = b.sem("d_s")
    d_t = b.sem("d_t")
    d_fin = b.sem("d_fin")
    o_st0 = b.sem("o_st0")
    o_st1 = b.sem("o_st1")

    def barrier():
        b.step("tensor", lambda t: t.ldweights(ident[:, :]), inc=bar, amt=1)
        b.step("vector", lambda v: v.memset(cst[:, 5:6], 0.0), inc=bar, amt=1)
        b.step("scalar", lambda s: s.activation(out=cst[:, 6:7], in_=cst[:, 6:7],
                                                func=AF.Copy), inc=bar, amt=1)
        b.step("gpsimd", lambda g: g.memset(cst[:, 7:8], 0.0), inc=bar, amt=1)
        b.step("sync", lambda s: s.nop(), inc=bar, amt=1)
        want = bar.n
        for e in ENGS:
            b.step(e, lambda x: x.nop(), waits=[(bar, want)])

    def ph0():
        def f_sync(s, sem):
            s.dma_start(out=tri[:, :], in_=tri_hbm).then_inc(sem, 16)
            s.dma_start(out=ident[:, :], in_=id_hbm).then_inc(sem, 16)
            s.dma_start(out=qg_rep[:, :], in_=qng_hbm.broadcast_to((128, D))).then_inc(sem, 16)
            s.dma_start(out=qb_rep[:, :], in_=qnb_hbm.broadcast_to((128, D))).then_inc(sem, 16)
            s.dma_start(out=kg_rep[:, :], in_=kng_hbm.broadcast_to((128, D))).then_inc(sem, 16)
            s.dma_start(out=kb_rep[:, :], in_=knb_hbm.broadcast_to((128, D))).then_inc(sem, 16)
        b.step("sync", f_sync, inc=s_ld, amt=6 * 16, self_inc=True)
        tok_small = (s_ld, s_ld.n)

        def f_gp(g, sem):
            src = wqkv_hbm.rearrange("(c p) f -> p c f", p=128)
            for i in range(4):
                g.dma_start(out=wqkv_sb[:, 4 * i:4 * i + 4, :],
                            in_=src[:, 4 * i:4 * i + 4, :]).then_inc(sem, 16)
            g.dma_start(out=wo_sb[:, :, :],
                        in_=wo_hbm.rearrange("(c p) f -> p c f", p=128)).then_inc(sem, 16)
        b.step("gpsimd", f_gp, inc=g_ld, amt=5 * 16, self_inc=True)
        tok_w = (g_ld, g_ld.n)

        def f_vec(v):
            inst = None
            for i, val in enumerate((C_EPS, ISD, C_PI2, C_NEG1, C_IVF, 0.0, 0.0, 0.0)):
                inst = v.memset(cst[:, i:i + 1], val)
            return inst
        b.step("vector", f_vec, inc=a_v)
        tok_cst = (a_v, a_v.n)

        def f_gp2(s, sem):
            s.dma_start(out=invfreq_i[:, :], in_=iot_hbm.broadcast_to((128, 64))
                        ).then_inc(sem, 16)
            s.dma_start(out=tcol[:, :], in_=tcol_hbm).then_inc(sem, 16)
        b.step("sync", f_gp2, inc=s_iot, amt=32, self_inc=True)
        tok_if = (s_iot, s_iot.n)

        def f_scl(s):
            return s.activation(out=invfreq[:, :], in_=invfreq_i[:, :], func=AF.Exp,
                                scale=cst[:, 4:5])
        b.step("scalar", f_scl, waits=[tok_if, tok_cst], inc=a_s)
        return tok_small, tok_w, (a_s, a_s.n)

    tok_small, tok_w, tok_if = ph0()

    def phaseA(ntiles=NTT, cut=99):
        t_tok = tok_if
        for tt in range(NTT):
            bf = tt % 2

            def f_vr(v, tt=tt, bf=bf):
                r2 = r2_s[:, bf, :]

                def reduce_angle(ang, o):
                    u = r2[:, o:o + 64]
                    ki = r2[:, o + 64:o + 128].bitcast(I32)
                    kf = r2[:, o + 128:o + 192]
                    v.tensor_scalar_mul(out=u, in0=ang, scalar1=1.0 / TWO_PI)
                    v.drain()
                    v.tensor_copy(out=ki, in_=u)
                    v.drain()
                    v.tensor_copy(out=kf, in_=ki)
                    phi = r2[:, o + 192:o + 256]
                    v.drain()
                    v.scalar_tensor_tensor(out=phi, in0=kf, scalar=-TWO_PI, in1=ang,
                                           op0=ALU.mult, op1=ALU.add)
                    m = r2[:, o:o + 64]
                    v.drain()
                    v.tensor_single_scalar(out=m, in_=phi, scalar=math.pi, op=ALU.is_gt)
                    v.drain()
                    v.scalar_tensor_tensor(out=phi, in0=m, scalar=-TWO_PI, in1=phi,
                                           op0=ALU.mult, op1=ALU.add)
                    v.drain()
                    v.tensor_single_scalar(out=m, in_=phi, scalar=-math.pi, op=ALU.is_lt)
                    v.drain()
                    return v.scalar_tensor_tensor(out=phi, in0=m, scalar=TWO_PI, in1=phi,
                                                  op0=ALU.mult, op1=ALU.add)
                th = r1_s[:, bf, 0:64]
                v.tensor_scalar(out=th, in0=invfreq[:, :], scalar1=tcol[:, tt:tt + 1],
                                scalar2=None, op0=ALU.mult)
                v.drain()
                reduce_angle(th, 0)
                ac = r1_s[:, bf, 64:128]
                v.drain()
                v.tensor_scalar_add(out=ac, in0=th, scalar1=C_PI2)
                v.drain()
                return reduce_angle(ac, 256)
            prev = (a_s, a_s.n) if tt >= 2 else None
            ang_tok = b.step("vector", f_vr,
                             waits=[t_tok, tok_if] + ([prev] if prev else []), inc=a_v)

            def f_sr(sc, tt=tt, bf=bf):
                phi = r2_s[:, bf, 192:256]
                psi = r2_s[:, bf, 448:512]
                sc.activation(out=sin_tab[:, tt, 0:64], in_=phi, func=AF.Sin,
                              scale=cst[:, 3:4])
                sc.activation(out=sin_tab[:, tt, 64:128], in_=phi, func=AF.Sin)
                cdup = cos_tab[:, tt, :].rearrange("p (r i) -> p r i", r=2)
                psb = psi.unsqueeze(1).broadcast_to((128, 2, 64))
                return sc.activation(out=cdup, in_=psb, func=AF.Sin)
            b.step("scalar", f_sr, waits=[ang_tok], inc=a_s)
        rope_done = (a_s, a_s.n)

        pb0_free = {0: None, 1: None}
        pb1_free = None
        for tt in range(ntiles):
            bf = tt % 2
            blanket = (a_fin, 3 * (tt - 1)) if tt >= 2 else None
            bl = [blanket] if blanket else []

            def f_x(s, tt=tt, bf=bf):
                src = x_hbm.rearrange("(n p) d -> p n d", p=128)
                return s.dma_start(out=x_sb[:, bf, :], in_=src[:, tt, :])
            x_tok = b.step("sync", f_x, waits=bl, inc=(a_x0 if bf == 0 else a_x1),
                           amt=16)

            last = {"tensor": None, "vector": None, "scalar": None, "gpsimd": None}
            if cut < 2:
                b.step("sync", lambda s, x_tok=x_tok: s.nop(), waits=[x_tok])
                for e in ("tensor", "vector", "scalar", "gpsimd"):
                    b.step(e, lambda x: x.nop(), waits=[x_tok], inc=a_fin)
                continue
            def f_v1(v, bf=bf):
                return v.reduce_sum(out=st_a[:, bf, 0:1], in_=x_sb[:, bf, :], axis=AX)
            sum_tok = b.step("vector", f_v1, waits=[x_tok], inc=a_v)

            def f_s1(s, bf=bf):
                return s.activation(out=sqx[:, :], in_=x_sb[:, bf, :], func=AF.Square,
                                    accum_out=st_a[:, bf, 1:2])
            ssq_tok = b.step("scalar", f_s1, waits=[x_tok], inc=a_s)

            def f_v2(v, bf=bf):
                stx = st_a[:, bf, :]
                v.tensor_scalar_mul(out=stx[:, 2:3], in0=stx[:, 0:1], scalar1=1.0 / HID)
                v.drain()
                v.tensor_tensor(out=stx[:, 3:4], in0=stx[:, 2:3], in1=stx[:, 2:3],
                                op=ALU.mult)
                v.drain()
                return v.scalar_tensor_tensor(out=stx[:, 4:5], in0=stx[:, 1:2],
                                              scalar=1.0 / HID, in1=stx[:, 3:4],
                                              op0=ALU.mult, op1=ALU.subtract)
            var_tok = b.step("vector", f_v2, waits=[ssq_tok, sum_tok], inc=a_v)
            if cut < 3:
                b.step("tensor", lambda t: t.nop(), waits=[var_tok], inc=a_fin)
                b.step("vector", lambda v: v.nop(), waits=[var_tok], inc=a_fin)
                b.step("scalar", lambda s: s.nop(), waits=[ssq_tok], inc=a_fin)
                b.step("gpsimd", lambda g: g.nop(), waits=[var_tok], inc=a_fin)
                continue

            def f_s2(s, bf=bf):
                return s.activation(out=st_a[:, bf, 5:6], in_=st_a[:, bf, 4:5],
                                    func=AF.Sqrt, bias=cst[:, 0:1])
            std_tok = b.step("scalar", f_s2, waits=[var_tok], inc=a_s)

            def f_v3(v, bf=bf):
                v.reciprocal(out=st_a[:, bf, 6:7], in_=st_a[:, bf, 5:6])
                v.drain()
                return v.tensor_scalar(out=st_a[:, bf, 7:8], in0=st_a[:, bf, 2:3],
                                       scalar1=st_a[:, bf, 6:7], scalar2=-1.0,
                                       op0=ALU.mult, op1=ALU.mult)
            inv_tok = b.step("vector", f_v3, waits=[std_tok], inc=a_v)

            def f_g1(sc, bf=bf):
                return sc.activation(out=xn_tm[:, bf, :], in_=x_sb[:, bf, :],
                                     func=AF.Identity, scale=st_a[:, bf, 6:7],
                                     bias=st_a[:, bf, 7:8])
            xn_tok = b.step("scalar", f_g1, waits=[inv_tok], inc=a_s)
            if cut < 5:
                b.step("tensor", lambda t: t.nop(), waits=[xn_tok], inc=a_fin)
                b.step("vector", lambda v: v.nop(), waits=[xn_tok], inc=a_fin)
                b.step("scalar", lambda s: s.nop(), waits=[std_tok], inc=a_fin)
                b.step("gpsimd", lambda g: g.nop(), waits=[xn_tok], inc=a_fin)
                continue

            grp_toks = []
            for grp in range(4):
                half = grp % 2

                def f_tr(t, bf=bf, grp=grp, half=half):
                    for j in range(4):
                        dc = grp * 4 + j
                        inst = t.transpose(out=pbank[half][:, 128 * j:128 * j + 128],
                                           in_=xn_tm[:, bf, 128 * dc:128 * dc + 128],
                                           identity=ident[:, :])
                    return inst
                w = [xn_tok, tok_small]
                if pb0_free[half] is not None:
                    w.append(pb0_free[half])
                tr_tok = b.step("tensor", f_tr, waits=w, inc=a_t)

                def f_ev(v, bf=bf, grp=grp, half=half):
                    return v.tensor_copy(
                        out=xn_fm[:, bf, grp * 4:(grp + 1) * 4, :],
                        in_=pbank[half][:, 0:512].rearrange("p (c i) -> p c i", c=4))
                ev_tok = b.step("vector", f_ev, waits=[tr_tok], inc=a_v)
                pb0_free[half] = ev_tok
                grp_toks.append(ev_tok)

            bq = pf[0 + bf]
            bkv = pf[2 + bf]

            def f_mm(t, bf=bf, bq=bq, bkv=bkv):
                for dc in range(NDC):
                    lhs = xn_fm[:, bf, dc, :]
                    t.matmul(out=bq[:, :], lhsT=lhs, rhs=wqkv_sb[:, dc, 0:512],
                             start=(dc == 0), stop=(dc == NDC - 1))
                    inst = t.matmul(out=bkv[:, 0:256], lhsT=lhs,
                                    rhs=wqkv_sb[:, dc, 512:768],
                                    start=(dc == 0), stop=(dc == NDC - 1))
                return inst
            if cut < 6:
                b.step("tensor", lambda t, w=grp_toks[-1]: t.nop(), waits=[grp_toks[-1]], inc=a_fin)
                b.step("vector", lambda v: v.nop(), waits=[grp_toks[-1]], inc=a_fin)
                b.step("scalar", lambda s: s.nop(), waits=[std_tok], inc=a_fin)
                b.step("gpsimd", lambda g: g.nop(), waits=[xn_tok], inc=a_fin)
                continue
            mm_tok = b.step("tensor", f_mm, waits=[grp_toks[-1], tok_w], inc=a_t)

            if cut < 7:
                b.step("tensor", lambda t: t.nop(), waits=[mm_tok], inc=a_fin)
                b.step("vector", lambda v: v.nop(), waits=[mm_tok], inc=a_fin)
                b.step("scalar", lambda s: s.nop(), waits=[std_tok], inc=a_fin)
                b.step("gpsimd", lambda g: g.nop(), waits=[xn_tok], inc=a_fin)
                continue
            def f_v5(v, bf=bf, tt=tt, bq=bq, bkv=bkv):
                v.tensor_copy(out=v_tm[:, tt, :], in_=bkv[:, 128:256])
                stx = st_a[:, bf, :]
                v.reduce_sum(out=stx[:, 8:12],
                             in_=bq[:, :].rearrange("p (h d) -> p h d", h=4), axis=AX)
                return v.reduce_sum(out=stx[:, 12:13], in_=bkv[:, 0:128], axis=AX)
            qsum_tok = b.step("vector", f_v5, waits=[mm_tok], inc=a_v)
            if cut < 62:
                b.step("tensor", lambda t: t.nop(), waits=[qsum_tok], inc=a_fin)
                b.step("vector", lambda v: v.nop(), waits=[qsum_tok], inc=a_fin)
                b.step("scalar", lambda s: s.nop(), waits=[std_tok], inc=a_fin)
                b.step("gpsimd", lambda g: g.nop(), waits=[xn_tok], inc=a_fin)
                continue

            def f_s3(s, bf=bf, bq=bq, bkv=bkv):
                s.activation(out=sqd[:, bf, 0:512], in_=bq[:, :], func=AF.Square)
                return s.activation(out=sqd[:, bf, 512:640], in_=bkv[:, 0:128],
                                    func=AF.Square)
            sq_tok = b.step("scalar", f_s3, waits=[mm_tok, qsum_tok], inc=a_s)
            if cut < 63:
                b.step("tensor", lambda t: t.nop(), waits=[sq_tok], inc=a_fin)
                b.step("vector", lambda v: v.nop(), waits=[qsum_tok], inc=a_fin)
                b.step("scalar", lambda s: s.nop(), waits=[sq_tok], inc=a_fin)
                b.step("gpsimd", lambda g: g.nop(), waits=[xn_tok], inc=a_fin)
                continue

            def f_v6(v, bf=bf):
                stx = st_a[:, bf, :]
                v.reduce_sum(out=stx[:, 13:18],
                             in_=sqd[:, bf, :].rearrange("p (h d) -> p h d", h=5),
                             axis=AX)
                v.tensor_scalar_mul(out=stx[:, 18:23], in0=stx[:, 8:13], scalar1=1.0 / D)
                v.drain()
                v.tensor_tensor(out=stx[:, 23:28], in0=stx[:, 18:23], in1=stx[:, 18:23],
                                op=ALU.mult)
                v.drain()
                return v.scalar_tensor_tensor(out=stx[:, 28:33], in0=stx[:, 13:18],
                                              scalar=1.0 / D, in1=stx[:, 23:28],
                                              op0=ALU.mult, op1=ALU.subtract)
            qvar_tok = b.step("vector", f_v6, waits=[sq_tok, qsum_tok], inc=a_v)
            if cut < 64:
                b.step("tensor", lambda t: t.nop(), waits=[qvar_tok], inc=a_fin)
                b.step("vector", lambda v: v.nop(), waits=[qvar_tok], inc=a_fin)
                b.step("scalar", lambda s: s.nop(), waits=[sq_tok], inc=a_fin)
                b.step("gpsimd", lambda g: g.nop(), waits=[xn_tok], inc=a_fin)
                continue

            def f_s4(s, bf=bf):
                return s.activation(out=st_a[:, bf, 33:38], in_=st_a[:, bf, 28:33],
                                    func=AF.Sqrt, bias=cst[:, 0:1])
            qstd_tok = b.step("scalar", f_s4, waits=[qvar_tok], inc=a_s)
            if cut < 65:
                b.step("tensor", lambda t: t.nop(), waits=[qvar_tok], inc=a_fin)
                b.step("vector", lambda v: v.nop(), waits=[qvar_tok], inc=a_fin)
                b.step("scalar", lambda s: s.nop(), waits=[qstd_tok], inc=a_fin)
                b.step("gpsimd", lambda g: g.nop(), waits=[xn_tok], inc=a_fin)
                continue

            def f_v7(v, bf=bf, bq=bq, bkv=bkv):
                stx = st_a[:, bf, :]
                v.reciprocal(out=stx[:, 23:28], in_=stx[:, 33:38])
                v.drain()
                inst = None
                for seg in range(5):
                    src = bq[:, 128 * seg:128 * seg + 128] if seg < 4 else bkv[:, 0:128]
                    inst = v.tensor_scalar(out=xc_s[:, bf, 128 * seg:128 * seg + 128],
                                           in0=src,
                                           scalar1=stx[:, 18 + seg:19 + seg],
                                           scalar2=stx[:, 23 + seg:24 + seg],
                                           op0=ALU.subtract, op1=ALU.mult)
                return inst
            xc_tok = b.step("vector", f_v7, waits=[qstd_tok], inc=a_v)
            if cut < 8:
                b.step("tensor", lambda t: t.nop(), waits=[xc_tok], inc=a_fin)
                b.step("vector", lambda v: v.nop(), waits=[xc_tok], inc=a_fin)
                b.step("scalar", lambda s: s.nop(), waits=[qstd_tok], inc=a_fin)
                b.step("gpsimd", lambda g: g.nop(), waits=[xc_tok], inc=a_fin)
                continue

            def f_g2(v, bf=bf):
                gq = qg_rep[:, :].unsqueeze(1).broadcast_to((128, 4, D))
                bqr = qb_rep[:, :].unsqueeze(1).broadcast_to((128, 4, D))
                xq = xc_s[:, bf, 0:512].rearrange("p (h d) -> p h d", h=4)
                yq = y_s[:, bf, 0:512].rearrange("p (h d) -> p h d", h=4)
                v.tensor_tensor(out=yq, in0=xq, in1=gq, op=ALU.mult)
                v.tensor_tensor(out=y_s[:, bf, 512:640], in0=xc_s[:, bf, 512:640],
                                in1=kg_rep[:, :], op=ALU.mult)
                v.drain()
                v.tensor_tensor(out=yq, in0=yq, in1=bqr, op=ALU.add)
                return v.tensor_tensor(out=y_s[:, bf, 512:640], in0=y_s[:, bf, 512:640],
                                       in1=kb_rep[:, :], op=ALU.add)
            y_tok = b.step("vector", f_g2, waits=[xc_tok, tok_small], inc=a_v)

            def f_v8(v, bf=bf, tt=tt):
                cos5 = cos_tab[:, tt, :].unsqueeze(1).broadcast_to((128, 5, D))
                sin5 = sin_tab[:, tt, :].unsqueeze(1).broadcast_to((128, 5, D))
                yv = y_s[:, bf, :].rearrange("p (s d) -> p s d", s=5)
                yrot = y_s[:, bf, :].rearrange("p (s j i) -> p s j i", s=5, j=2)[:, :, ::-1, :]
                t1 = r1_s[:, bf, :].rearrange("p (s d) -> p s d", s=5)
                t2 = r2_s[:, bf, :].rearrange("p (s j i) -> p s j i", s=5, j=2)
                sin4 = sin5.rearrange("p s (j i) -> p s j i", j=2)
                v.tensor_tensor(out=t1, in0=yv, in1=cos5, op=ALU.mult)
                v.tensor_tensor(out=t2, in0=yrot, in1=sin4, op=ALU.mult)
                v.drain()
                return v.tensor_tensor(out=qk_tm[:, bf, :].rearrange("p (s d) -> p s d", s=5),
                                       in0=t1, in1=t2, op=ALU.add)
            qk_tok = b.step("vector", f_v8, waits=[y_tok, rope_done], inc=a_v)
            if cut < 10:
                b.step("tensor", lambda t: t.nop(), waits=[qk_tok], inc=a_fin)
                b.step("vector", lambda v: v.nop(), waits=[qk_tok], inc=a_fin)
                b.step("scalar", lambda s: s.nop(), waits=[qstd_tok], inc=a_fin)
                continue

            def f_t3(t, bf=bf):
                for seg in range(5):
                    inst = t.transpose(out=pf4b[:, 128 * seg:128 * seg + 128],
                                       in_=qk_tm[:, bf, 128 * seg:128 * seg + 128],
                                       identity=ident[:, :])
                return inst
            w = [qk_tok]
            if pb1_free is not None:
                w.append(pb1_free)
            qkt_tok = b.step("tensor", f_t3, waits=w, inc=a_t)

            def f_v9(v, tt=tt):
                v.tensor_copy(out=q_fm[:, 0:4, tt, :],
                              in_=pf4b[:, 0:512].rearrange("p (h i) -> p h i", h=4))
                return v.tensor_copy(out=k_fm[:, 128 * tt:128 * tt + 128],
                                     in_=pf4b[:, 512:640])
            fm_tok = b.step("vector", f_v9, waits=[qkt_tok], inc=a_v)
            pb1_free = fm_tok

            b.step("tensor", lambda t: t.nop(), waits=[qkt_tok], inc=a_fin)
            b.step("vector", lambda v: v.nop(), waits=[fm_tok], inc=a_fin)
            b.step("scalar", lambda s: s.nop(), waits=[qstd_tok, sq_tok], inc=a_fin)
            
    if stage.startswith("An:"):
        phaseA(int(stage.split(":")[1]), cut)
        barrier()
        _dbg_dump(b, [("d_cos", cos_tab[:, :, :], [128, NTT, 128], BF16),
                      ("d_sin", sin_tab[:, :, :], [128, NTT, 128], BF16),
                      ("d_if", invfreq[:, :], [128, 64], F32),
                      ("d_tcol", tcol[:, :], [128, NTT], F32),
                      ("d_r2a", r2_s[:, 0, :], [128, 640], F32),
                      ("d_r2b", r2_s[:, 1, :], [128, 640], F32),
                      ("d_r1a", r1_s[:, 0, :], [128, 640], F32),
                      ("d_r1b", r1_s[:, 1, :], [128, 640], F32)])
        return _finish(b, nc)
    phaseA()
    if stage == "A":
        barrier()
        _dbg_dump(b, [("d_qfm", q_fm[:, 0, :, :], [128, NTT * 128], BF16),
                      ("d_qfm3", q_fm[:, 3, :, :], [128, NTT * 128], BF16),
                      ("d_kfm", k_fm[:, :], [128, NTT * 128], BF16),
                      ("d_vtm", v_tm[:, :, :], [128, NTT * 128], BF16),
                      ("d_cos", cos_tab[:, :, :], [128, NTT * 128], BF16),
                      ("d_sin", sin_tab[:, :, :], [128, NTT * 128], BF16)])
        return _finish(b, nc)
    barrier()

    def phaseB(nunits=4 * NTT, bcut=9):
        bar_tok = (bar, bar.n)
        s1_free = {0: bar_tok, 1: bar_tok}
        s2_free = {0: bar_tok, 1: bar_tok}
        ptb_free = {0: bar_tok, 1: bar_tok}
        p_free = {0: bar_tok, 1: bar_tok}
        pt_free = {0: bar_tok, 1: bar_tok}
        ab_free = bar_tok
        atr_free = bar_tok

        def emit_pass1(u):
            h, qb = u // NTT, u % NTT
            nkt = qb // 4 + 1
            kws = [512] * (nkt - 1) + [128 * (qb % 4) + 128]
            us = u % 4
            blanket = (b_fin, 3 * (u - 3)) if u >= 4 else None
            bl = [blanket] if blanket else []

            for kt in range(nkt):
                kw = kws[kt]
                diag = kt == nkt - 1

                def f_s1mm(t, h=h, qb=qb, kt=kt, kw=kw):
                    return t.matmul(out=pf[kt % 2][:, 0:kw], lhsT=q_fm[:, h, qb, :],
                                    rhs=k_fm[:, 512 * kt:512 * kt + kw],
                                    start=True, stop=True)
                s1_tok = b.step("tensor", f_s1mm, waits=bl + [s1_free[kt % 2]], inc=b_t)

                def f_mx(v, kt=kt, kw=kw, diag=diag, us=us):
                    pk = pf[kt % 2]
                    if diag:
                        v.tensor_tensor(out=pk[:, kw - 128:kw], in0=pk[:, kw - 128:kw],
                                        in1=tri[:, :], op=ALU.add)
                        v.drain()
                    return v.reduce_max(out=mcol[:, us, kt:kt + 1], in_=pk[:, 0:kw],
                                        axis=AX)
                tok = b.step("vector", f_mx, waits=bl + [s1_tok], inc=b_v)
                s1_free[kt % 2] = tok

            def f_m(v, us=us, nkt=nkt):
                v.reduce_max(out=mrow[:, us, 2:3], in_=mcol[:, us, 0:nkt], axis=AX)
                v.drain()
                return v.tensor_scalar_mul(out=mrow[:, us, 0:1], in0=mrow[:, us, 2:3],
                                           scalar1=-ISD)
            m_tok = b.step("vector", f_m, inc=b_v)
            return dict(h=h, qb=qb, nkt=nkt, kws=kws, us=us, bl=bl, m_tok=m_tok)

        def emit_pass2(st):
            h, qb, nkt, kws, us, bl, m_tok = (st[k] for k in
                                              ("h", "qb", "nkt", "kws", "us", "bl", "m_tok"))
            nonlocal ab_free, atr_free
            v_last = None
            pend = []

            def emit_trv(kt, kw, ks, exp_tok, nch):
                nonlocal v_last

                def f_tr(t, kw=kw, ks=ks, nch=nch):
                    for ci in range(nch):
                        inst = t.transpose(out=pbank[ks][:, 128 * ci:128 * ci + 128],
                                           in_=p_sb[:, ks, 128 * ci:128 * ci + 128],
                                           identity=ident[:, :])
                    return inst
                tr_tok = b.step("tensor", f_tr, waits=[exp_tok, ptb_free[ks]], inc=b_t)
                p_free[ks] = tr_tok

                if kt % 2 == 0:
                    def f_ev(v, kw=kw, ks=ks):
                        return v.tensor_copy(out=pt_sb[:, ks, 0:kw],
                                             in_=pbank[ks][:, 0:kw])
                    ev_tok = b.step("vector", f_ev, waits=[tr_tok, pt_free[ks]], inc=b_v)
                else:
                    def f_ev(sc, kw=kw, ks=ks):
                        return sc.activation(out=pt_sb[:, ks, 0:kw],
                                             in_=pbank[ks][:, 0:kw],
                                             func=AF.Copy)
                    ev_tok = b.step("scalar", f_ev, waits=[tr_tok, pt_free[ks]], inc=b_s)
                ptb_free[ks] = ev_tok

                vw = [ev_tok]
                if kt == 0:
                    vw.append(ab_free)

                def f_vmm(t, kt=kt, ks=ks, nch=nch, nkt=nkt):
                    for ci in range(nch):
                        inst = t.matmul(out=pf[4][:, 0:128],
                                        lhsT=pt_sb[:, ks, 128 * ci:128 * ci + 128],
                                        rhs=v_tm[:, 4 * kt + ci, :],
                                        start=(kt == 0 and ci == 0),
                                        stop=(kt == nkt - 1 and ci == nch - 1))
                    return inst
                v_tok = b.step("tensor", f_vmm, waits=vw, inc=b_t)
                pt_free[ks] = v_tok
                v_last = v_tok

            for kt in range(nkt):
                kw = kws[kt]
                diag = kt == nkt - 1
                ks = kt % 2

                def f_s2mm(t, h=h, qb=qb, kt=kt, kw=kw, ks=ks):
                    return t.matmul(out=pf[2 + ks][:, 0:kw], lhsT=q_fm[:, h, qb, :],
                                    rhs=k_fm[:, 512 * kt:512 * kt + kw],
                                    start=True, stop=True)
                s2_tok = b.step("tensor", f_s2mm, waits=bl + [s2_free[ks]], inc=b_t)

                exp_w = [s2_tok, m_tok, p_free[ks]]
                if diag:
                    def f_mk2(v, kw=kw, ks=ks):
                        pk = pf[2 + ks]
                        return v.tensor_tensor(out=pk[:, kw - 128:kw],
                                               in0=pk[:, kw - 128:kw],
                                               in1=tri[:, :], op=ALU.add)
                    mk2_tok = b.step("vector", f_mk2, waits=[s2_tok], inc=b_v)
                    exp_w.append(mk2_tok)

                def f_exp(sc, kt=kt, kw=kw, ks=ks, us=us):
                    return sc.activation(out=p_sb[:, ks, 0:kw], in_=pf[2 + ks][:, 0:kw],
                                         func=AF.Exp, scale=cst[:, 1:2],
                                         bias=mrow[:, us, 0:1],
                                         accum_out=lcol[:, us, kt:kt + 1])
                exp_tok = b.step("scalar", f_exp, waits=exp_w, inc=b_s)
                s2_free[ks] = exp_tok

                pend.append((kt, kw, ks, exp_tok, kw // 128))
                if len(pend) >= 2:
                    emit_trv(*pend.pop(0))
            while pend:
                emit_trv(*pend.pop(0))

            def f_l(v, us=us, nkt=nkt):
                v.reduce_sum(out=mrow[:, us, 3:4], in_=lcol[:, us, 0:nkt], axis=AX)
                v.drain()
                return v.reciprocal(out=mrow[:, us, 1:2], in_=mrow[:, us, 3:4])
            l_tok = b.step("vector", f_l, waits=[(b_s, b_s.n)], inc=b_v)

            def f_ae(v, us=us):
                return v.tensor_scalar_mul(out=atm[:, us, :], in0=pf[4][:, 0:128],
                                           scalar1=mrow[:, us, 1:2])
            ae_tok = b.step("vector", f_ae, waits=[l_tok, v_last], inc=b_v)
            ab_free = ae_tok

            def f_at(t, us=us):
                return t.transpose(out=pf5b[:, 0:128], in_=atm[:, us, :],
                                   identity=ident[:, :])
            at_tok = b.step("tensor", f_at, waits=[ae_tok, atr_free], inc=b_t)

            def f_af(v, h=h, qb=qb):
                return v.tensor_copy(out=attn_fm[:, h, qb, :], in_=pf5b[:, 0:128])
            af_tok = b.step("vector", f_af, waits=[at_tok], inc=b_v)
            atr_free = af_tok

            b.step("tensor", lambda t: t.nop(), waits=[at_tok], inc=b_fin)
            b.step("vector", lambda v: v.nop(), waits=[af_tok], inc=b_fin)
            b.step("scalar", lambda s: s.nop(), waits=[(b_s, b_s.n)], inc=b_fin)

        prev = None
        for u in range(nunits):
            st = emit_pass1(u)
            if prev is not None:
                emit_pass2(prev)
            prev = st
        if prev is not None:
            emit_pass2(prev)

    if stage.startswith("Bn:"):
        parts = stage.split(":")
        phaseB(int(parts[1]), int(parts[2]) if len(parts) > 2 else 9)
        barrier()
        _dbg_dump(b, [("d_attn", attn_fm[:, 0, :, :], [128, NTT * 128], BF16)])
        return _finish(b, nc)
    phaseB()
    if stage == "B":
        barrier()
        _dbg_dump(b, [("d_attn", attn_fm[:, 0, :, :], [128, NTT * 128], BF16),
                      ("d_attn3", attn_fm[:, 3, :, :], [128, NTT * 128], BF16)])
        return _finish(b, nc)
    barrier()

    def phaseB2():
        bar_tok = (bar, bar.n)
        bank_free = {0: bar_tok, 1: bar_tok}
        po_free = {0: bar_tok, 1: bar_tok}
        for i, (os_, tt) in enumerate([(a, t) for a in range(4) for t in range(NTT)]):
            sl = i % 2

            def f_mm(t, os_=os_, tt=tt, sl=sl):
                for fc in range(4):
                    inst = t.matmul(out=pf[sl][:, :], lhsT=attn_fm[:, fc, tt, :],
                                    rhs=wo_sb[:, fc, 512 * os_:512 * os_ + 512],
                                    start=(fc == 0), stop=(fc == 3))
                return inst
            mm_tok = b.step("tensor", f_mm, waits=[bank_free[sl]], inc=c_t)

            if i % 2 == 0:
                def f_ev(v, sl=sl):
                    return v.tensor_copy(out=po_sb[:, sl, :], in_=pf[sl][:, :])
                ev_tok = b.step("vector", f_ev, waits=[mm_tok, po_free[sl]], inc=c_v)
            else:
                def f_ev(s, sl=sl):
                    return s.activation(out=po_sb[:, sl, :], in_=pf[sl][:, :],
                                        func=AF.Copy)
                ev_tok = b.step("scalar", f_ev, waits=[mm_tok, po_free[sl]], inc=c_s)
            bank_free[sl] = ev_tok

            def f_st(s, os_=os_, tt=tt, sl=sl):
                return s.dma_start(out=ccin_hbm[128 * tt:128 * tt + 128,
                                                512 * os_:512 * os_ + 512],
                                   in_=po_sb[:, sl, :])
            st_tok = b.step("sync", f_st, waits=[ev_tok],
                            inc=(c_st0 if sl == 0 else c_st1), amt=16)
            po_free[sl] = st_tok

        def f_cc(g):
            return g.collective_compute(
                "ReduceScatter", ALU.add, replica_groups=rg,
                ins=[ccin_hbm[:, :]], outs=[ccout_hbm[:, :]])
        b.step("gpsimd", f_cc, waits=[(c_st0, c_st0.n), (c_st1, c_st1.n)],
               inc=cc_done)

    phaseB2()
    barrier()
    cc_tok = (cc_done, cc_done.n)

    def phaseD(d0_only=False):
        bar_tok = (bar, bar.n)

        def f_sc(s, sem):
            s.dma_start(out=sg_fm[:, :],
                        in_=sg_hbm.rearrange("o (c p) -> p (o c)", p=128),
                        allow_slow_non_contiguous=True).then_inc(sem, 16)
            s.dma_start(out=su_fm[:, :],
                        in_=su_hbm.rearrange("o (c p) -> p (o c)", p=128),
                        allow_slow_non_contiguous=True).then_inc(sem, 16)
        b.step("sync", f_sc, inc=d_sc, amt=2 * 16, self_inc=True)
        sc_tok = (d_sc, d_sc.n)

        pb0_free = {0: bar_tok, 1: bar_tok}
        x_free = bar_tok
        xn_toks = []
        for ot in range(8):
            bf = ot % 2

            def f_x(s, ot=ot):
                src = xown_hbm.rearrange("(n p) d -> p n d", p=128)
                return s.dma_start(out=xo_sb[:, :], in_=src[:, ot, :])
            x_tok = b.step("sync", f_x, waits=[x_free], inc=d_xo, amt=16)

            def f_v1(v, bf=bf):
                return v.reduce_sum(out=st_d[:, bf, 0:1], in_=xo_sb[:, :], axis=AX)
            sum_tok = b.step("vector", f_v1, waits=[x_tok], inc=d_v)

            def f_s1(s, bf=bf):
                return s.activation(out=sqd_d[:, :], in_=xo_sb[:, :], func=AF.Square,
                                    accum_out=st_d[:, bf, 1:2])
            ssq_tok = b.step("scalar", f_s1, waits=[x_tok], inc=d_s)

            def f_v2(v, bf=bf):
                stx = st_d[:, bf, :]
                v.tensor_scalar_mul(out=stx[:, 2:3], in0=stx[:, 0:1], scalar1=1.0 / HID)
                v.drain()
                v.tensor_tensor(out=stx[:, 3:4], in0=stx[:, 2:3], in1=stx[:, 2:3],
                                op=ALU.mult)
                v.drain()
                return v.scalar_tensor_tensor(out=stx[:, 4:5], in0=stx[:, 1:2],
                                              scalar=1.0 / HID, in1=stx[:, 3:4],
                                              op0=ALU.mult, op1=ALU.subtract)
            var_tok = b.step("vector", f_v2, waits=[ssq_tok, sum_tok], inc=d_v)

            def f_s2(s, bf=bf):
                return s.activation(out=st_d[:, bf, 5:6], in_=st_d[:, bf, 4:5],
                                    func=AF.Sqrt, bias=cst[:, 0:1])
            std_tok = b.step("scalar", f_s2, waits=[var_tok], inc=d_s)

            def f_v3(v, bf=bf):
                v.reciprocal(out=st_d[:, bf, 6:7], in_=st_d[:, bf, 5:6])
                v.drain()
                return v.tensor_scalar(out=xno_tm[:, bf, :], in0=xo_sb[:, :],
                                       scalar1=st_d[:, bf, 2:3],
                                       scalar2=st_d[:, bf, 6:7],
                                       op0=ALU.subtract, op1=ALU.mult)
            xn_tok = b.step("vector", f_v3, waits=[std_tok, ssq_tok], inc=d_v)
            x_free = xn_tok

            for grp in range(4):
                half = grp % 2

                def f_tr(t, bf=bf, grp=grp, half=half):
                    for j in range(4):
                        dc = grp * 4 + j
                        inst = t.transpose(out=pbank[half][:, 128 * j:128 * j + 128],
                                           in_=xno_tm[:, bf, 128 * dc:128 * dc + 128],
                                           identity=ident[:, :])
                    return inst
                tr_tok = b.step("tensor", f_tr, waits=[xn_tok, pb0_free[half]], inc=d_t)

                def f_ev(v, ot=ot, grp=grp, half=half):
                    return v.tensor_copy(
                        out=xn_own[:, grp * 4:(grp + 1) * 4, 128 * ot:128 * ot + 128],
                        in_=pbank[half][:, 0:512].rearrange("p (c i) -> p c i", c=4))
                ev_tok = b.step("vector", f_ev, waits=[tr_tok], inc=d_v)
                pb0_free[half] = ev_tok
                xn_toks.append(ev_tok)
        xn_done = xn_toks[-1]
        barrier()
        xn_done = (bar, bar.n)
        if d0_only:
            return

        wld_free = {0: [bar_tok], 1: [bar_tok]}
        wcast_free = {0: [bar_tok], 1: [bar_tok]}
        gs_free = {0: bar_tok, 1: bar_tok}
        up_free = {0: bar_tok, 1: bar_tok}
        bank_free = {id(pbf[0]): pb0_free[0], id(pbf[1]): pb0_free[1]}
        gu_last = None
        for hc in range(NHC):
            sl = hc % 2

            def f_wl(s, hc=hc):
                return s.dma_start(out=wgI[:, :, :],
                                   in_=wg_hbm.rearrange("(c p) f -> p c f", p=128
                                                        )[:, :, 128 * hc:128 * hc + 128])
            wl_tok = b.step("sync", f_wl, waits=wld_free[0], inc=d_wgu, amt=16)

            def f_wl2(sc, hc=hc):
                return sc.dma_start(out=wuI[:, :, :],
                                    in_=wu_hbm.rearrange("(c p) f -> p c f", p=128
                                                         )[:, :, 128 * hc:128 * hc + 128])
            wl2_tok = b.step("scalar", f_wl2, waits=wld_free[0], inc=d_wgu2, amt=16)

            def f_cg(v, sl=sl):
                return v.tensor_scalar_add(out=wg_sb[:, sl, :, :], in0=wgI[:, :, :],
                                           scalar1=-128.0)
            cg_tok = b.step("vector", f_cg, waits=[wl_tok] + wcast_free[sl], inc=d_v)

            def f_cu(s, sl=sl):
                return s.activation(out=wu_sb[:, sl, :, :], in_=wuI[:, :, :],
                                    func=AF.Copy, bias=-128.0)
            cu_tok = b.step("scalar", f_cu, waits=[wl2_tok] + wcast_free[sl], inc=d_s)
            wld_free[0] = [cg_tok, cu_tok]

            banks = [pf[0], pf[1], pf[2], pf[3]] if sl == 0 else \
                    [pf[4], pf[5], pbf[0], pbf[1]]
            for tsl in range(2):
                bg, bu = banks[tsl], banks[2 + tsl]

                def f_mm(t, sl=sl, tsl=tsl, bg=bg, bu=bu):
                    for dc in range(NDC):
                        t.matmul(out=bg[:, :], lhsT=wg_sb[:, sl, dc, :],
                                 rhs=xn_own[:, dc, 512 * tsl:512 * tsl + 512],
                                 start=(dc == 0), stop=(dc == NDC - 1))
                        inst = t.matmul(out=bu[:, :], lhsT=wu_sb[:, sl, dc, :],
                                        rhs=xn_own[:, dc, 512 * tsl:512 * tsl + 512],
                                        start=(dc == 0), stop=(dc == NDC - 1))
                    return inst
                mmw = [cg_tok, cu_tok, xn_done]
                for bk in (id(bg), id(bu)):
                    if bk in bank_free:
                        mmw.append(bank_free[bk])
                mm_tok = b.step("tensor", f_mm, waits=mmw, inc=d_t)

                def f_sil(s, hc=hc, tsl=tsl, bg=bg):
                    return s.activation(out=gs_tmp[:, tsl, :], in_=bg[:, :],
                                        func=AF.Silu, scale=sg_fm[:, hc:hc + 1])
                sil_tok = b.step("scalar", f_sil,
                                 waits=[mm_tok, sc_tok, gs_free[tsl]], inc=d_s)

                def f_up(v, hc=hc, tsl=tsl, bu=bu):
                    return v.tensor_scalar_mul(out=up_tmp[:, tsl, :], in0=bu[:, :],
                                               scalar1=su_fm[:, hc:hc + 1])
                up_tok = b.step("vector", f_up,
                                waits=[mm_tok, sc_tok, up_free[tsl]], inc=d_v)

                def f_gu(v, hc=hc, tsl=tsl):
                    return v.tensor_mul(out=gu_fm[:, hc, 512 * tsl:512 * tsl + 512],
                                        in0=gs_tmp[:, tsl, :], in1=up_tmp[:, tsl, :])
                gu_tok = b.step("vector", f_gu, waits=[sil_tok, up_tok], inc=d_v)
                bank_free[id(bg)] = gu_tok
                bank_free[id(bu)] = gu_tok
                gs_free[tsl] = gu_tok
                up_free[tsl] = gu_tok
                gu_last = gu_tok
            wcast_free[sl] = [mm_tok]

        banks8 = [pf[0], pf[1], pf[2], pf[3], pf[4], pf[5], pbf[0], pbf[1]]
        wd_free = {0: bar_tok, 1: bar_tok}
        dbank_free = {i: gu_last for i in range(8)}
        rs_free = {0: (d_v, 0), 1: (d_v, 0)}
        fin_free = {0: (d_v, 0), 1: (d_v, 0)}
        i = 0
        sd_free = (d_v, 0)
        for os_ in range(4):
            def f_sd(s, os_=os_):
                return s.dma_start(out=sd_rep[:, :],
                                   in_=sd_hbm[:, 512 * os_:512 * os_ + 512
                                              ].broadcast_to((128, 512)))
            sd_tok = b.step("sync", f_sd, waits=[sd_free], inc=d_sd, amt=16)

            mm_tok = None
            for hcb in range(11):
                sl = hcb % 2

                def f_wd(g, hcb=hcb, os_=os_, sl=sl):
                    src = wd_hbm[512 * hcb:512 * hcb + 512,
                                 512 * os_:512 * os_ + 512
                                 ].rearrange("(c p) o -> p c o", p=128)
                    return g.dma_start(out=wd_sb[:, sl, :, :], in_=src)
                wdl_tok = b.step("gpsimd", f_wd, waits=[wd_free[sl]],
                                 inc=(d_wd0 if sl == 0 else d_wd1), amt=16)

                def f_cd(v, sl=sl):
                    return v.tensor_scalar_add(out=wd_sb[:, sl, :, :],
                                               in0=wd_sb[:, sl, :, :], scalar1=-128.0)
                cd_tok = b.step("vector", f_cd, waits=[wdl_tok], inc=d_v)

                def f_mm(t, hcb=hcb, sl=sl):
                    for hq in range(4):
                        hc = 4 * hcb + hq
                        for tc in range(8):
                            inst = t.matmul(
                                out=banks8[tc][:, :],
                                lhsT=gu_fm[:, hc, 128 * tc:128 * tc + 128],
                                rhs=wd_sb[:, sl, hq, :],
                                start=(hc == 0), stop=(hc == NHC - 1))
                    return inst
                mmw = [cd_tok, gu_last]
                if hcb == 0:
                    mmw += [dbank_free[tc] for tc in range(8)]
                mm_tok = b.step("tensor", f_mm, waits=mmw, inc=d_t)
                wd_free[sl] = mm_tok

            for tc in range(8):
                sl = i % 2
                row0 = 128 * tc

                def f_ld(s, sem, row0=row0, os_=os_, sl=sl):
                    s.dma_start(out=rs_sb[:, sl, :],
                                in_=ccout_hbm[row0:row0 + 128,
                                              512 * os_:512 * os_ + 512]
                                ).then_inc(sem, 16)
                    s.dma_start(out=xot_sb[:, sl, :],
                                in_=xown_hbm[row0:row0 + 128,
                                             512 * os_:512 * os_ + 512]
                                ).then_inc(sem, 16)
                ld_tok = b.step("sync", f_ld, waits=[cc_tok, rs_free[sl]],
                                inc=(d_rs0 if sl == 0 else d_rs1), amt=32,
                                self_inc=True)

                def f_f1(v, tc=tc, sl=sl):
                    v.tensor_mul(out=out_sb[:, sl, :], in0=banks8[tc][:, :],
                                 in1=sd_rep[:, :])
                    v.drain()
                    v.tensor_add(out=out_sb[:, sl, :], in0=out_sb[:, sl, :],
                                 in1=xot_sb[:, sl, :])
                    v.drain()
                    return v.tensor_add(out=out_sb[:, sl, :], in0=out_sb[:, sl, :],
                                        in1=rs_sb[:, sl, :])
                f1_tok = b.step("vector", f_f1,
                                waits=[mm_tok, ld_tok, sd_tok, fin_free[sl]],
                                inc=d_v)
                dbank_free[tc] = f1_tok
                rs_free[sl] = f1_tok

                def f_out(s, row0=row0, os_=os_, sl=sl):
                    return s.dma_start(out=out_hbm[row0:row0 + 128,
                                                   512 * os_:512 * os_ + 512],
                                       in_=out_sb[:, sl, :])
                out_tok = b.step("sync", f_out, waits=[f1_tok],
                                 inc=(o_st0 if sl == 0 else o_st1), amt=16)
                fin_free[sl] = out_tok
                sd_free = f1_tok
                i += 1

        b.step("sync", lambda s: s.nop(),
               waits=[(o_st0, o_st0.n), (o_st1, o_st1.n)])

    if stage == "D0d":
        phaseD(d0_only=True)
        _dbg_dump(b, [("d_xn", xn_own[:, :, :], [128, NDC, OWN], BF16)])
        return _finish(b, nc)
    phaseD()
    if stage == "Dd":
        _dbg_dump(b, [("d_xn", xn_own[:, :, :], [128, NDC, OWN], BF16),
                      ("d_gu", gu_fm[:, :, :], [128, NHC, OWN], BF16),
                      ("d_rs", ccout_hbm[:, :], [OWN, HID], F32),
                      ("d_ci", ccin_hbm[:, :], [S, HID], F32)])
    return _finish(b, nc)


def _finish(b, nc):
    blk = nc.Block()
    block = blk.__enter__()
    b._ctxs.append(blk)

    @block.tensor
    def _(eng):
        b.replay("tensor", eng)

    @block.vector
    def _(eng):
        b.replay("vector", eng)

    @block.scalar
    def _(eng):
        b.replay("scalar", eng)

    @block.gpsimd
    def _(eng):
        b.replay("gpsimd", eng)

    @block.sync
    def _(eng):
        b.replay("sync", eng)

    blk.__exit__(None, None, None)
    b._ctxs.pop()
    return b


def _dbg_dump(b, specs):
    nc = b.nc
    dsem = b.sem("dbg_st")
    for name, ap, shape, dtype in specs:
        hbm = nc.dram_tensor(name, shape, dtype, kind="ExternalOutput").ap()
        b.dbg_outs.append(name)

        def f(s, sem, hbm=hbm, ap=ap):
            s.dma_start(out=hbm, in_=ap).then_inc(sem, 16)
        b.step("sync", f, waits=[], inc=dsem, amt=16, self_inc=True)
    b.step("sync", lambda s: s.nop(), waits=[(dsem, dsem.n)])


_BUILD_CACHE = {}


def _get_build(dbg=(), stage="full"):
    key = (tuple(sorted(dbg)), stage)
    if key not in _BUILD_CACHE:
        _BUILD_CACHE[key] = build(dbg, stage)
    return _BUILD_CACHE[key]


def make_in_maps(inputs):
    x = np.ascontiguousarray(np.asarray(inputs["L0_attn_x"], dtype=np.float32))
    w_qkv = np.asarray(inputs["L0_attn_w_qkv"], dtype=np.float32)
    w_o = np.asarray(inputs["L0_attn_w_o"], dtype=np.float32)
    qn_g = np.asarray(inputs["L0_attn_qn_g"], dtype=np.float32).reshape(1, D)
    qn_b = np.asarray(inputs["L0_attn_qn_b"], dtype=np.float32).reshape(1, D)
    kn_g = np.asarray(inputs["L0_attn_kn_g"], dtype=np.float32).reshape(1, D)
    kn_b = np.asarray(inputs["L0_attn_kn_b"], dtype=np.float32).reshape(1, D)
    wg_t = np.ascontiguousarray(np.asarray(inputs["L0_ffn_wq_gate"], dtype=np.int32).T)
    wu_t = np.ascontiguousarray(np.asarray(inputs["L0_ffn_wq_up"], dtype=np.int32).T)
    wd_t = np.ascontiguousarray(np.asarray(inputs["L0_ffn_wq_down"], dtype=np.int32).T)
    s_gate = np.asarray(inputs["L0_ffn_s_gate"], dtype=np.float32).reshape(1, FH)
    s_up = np.asarray(inputs["L0_ffn_s_up"], dtype=np.float32).reshape(1, FH)
    s_down = np.asarray(inputs["L0_ffn_s_down"], dtype=np.float32).reshape(1, HID)

    import ml_dtypes
    ident = np.eye(128, dtype=np.float32).astype(ml_dtypes.bfloat16)
    iot64 = np.arange(64, dtype=np.float32).reshape(1, 64)
    tcol_h = (np.arange(128, dtype=np.float32)[:, None]
              + 128.0 * np.arange(NTT, dtype=np.float32)[None, :])
    tcol_h = np.ascontiguousarray(tcol_h)
    tri = np.where(np.arange(128)[None, :] > np.arange(128)[:, None],
                   np.float32(NEG_BIG), np.float32(0.0)).astype(np.float32)

    wqkv_T = np.ascontiguousarray(w_qkv.T)
    wo_T = np.ascontiguousarray(w_o.T)

    in_maps = []
    for c in range(8):
        g, r = c // 4, c % 4
        cols = np.concatenate([
            np.arange(512 * r, 512 * r + 512),
            np.arange(2048 + 128 * r, 2048 + 128 * r + 128),
            np.arange(2560 + 128 * r, 2560 + 128 * r + 128),
        ])
        wqkv_slice = np.ascontiguousarray(wqkv_T[:, cols])
        wo_slice = np.ascontiguousarray(wo_T[512 * r:512 * r + 512, :])
        in_maps.append({
            "x": x[g],
            "x_own": np.ascontiguousarray(x[g, 1024 * r:1024 * (r + 1)]),
            "wqkv_t": wqkv_slice,
            "wo_t": wo_slice,
            "qn_g": qn_g, "qn_b": qn_b, "kn_g": kn_g, "kn_b": kn_b,
            "wg_t": wg_t, "wu_t": wu_t, "wd_t": wd_t,
            "s_gate": s_gate, "s_up": s_up, "s_down": s_down,
            "tri": tri, "ident": ident, "iot64": iot64, "tcol_h": tcol_h,
        })
    return in_maps


def kernel(**inputs):
    b = _get_build()
    in_maps = make_in_maps(inputs)
    res = run_bass_kernel_spmd(b.nc, in_maps, core_ids=list(range(8)))
    out = np.empty((2, S, HID), dtype=np.float32)
    for c in range(8):
        g, r = c // 4, c % 4
        out[g, 1024 * r:1024 * (r + 1)] = res.results[c]["out"]
    return out
